# version 10
# baseline (speedup 1.0000x reference)
import math

import numpy as np
import ml_dtypes

import concourse.bass as bass
import concourse.mybir as mybir
from concourse.bass_utils import run_bass_kernel_spmd

F32 = mybir.dt.float32
BF16 = mybir.dt.bfloat16
U8 = mybir.dt.uint8
ALU = mybir.AluOpType
AX = mybir.AxisListType
AF = mybir.ActivationFunctionType

B = 16
S = 2048
H = 1024
NH = 16
HD = 64
II = 2816
NCORES = 8
SEQ_PER_CORE = B // NCORES
TT = S // 128
FC = H // 128
ICC = II // 128
STRIDE = max(1, int(math.isqrt(S)))
EXPB = 12.0
SCALE = HD ** -0.5

NOLAG = False
RC0 = 1.5 * (2.0 ** 23) * 0.5
RC1 = 1.5 * (2.0 ** 23) * 1.0
RC2 = 1.5 * (2.0 ** 23) * 2.0

QCW = 512
NQC = S // QCW
CW = 1024


class Sem:
    __slots__ = ("name", "count", "h")

    def __init__(self, name):
        self.name = name
        self.count = 0
        self.h = None


def ts_(e, out, in0, s1, s2, op0, op1=None):
    if s2 is None:
        return e.tensor_single_scalar(out, in0, s1, op0)
    return e.tensor_scalar(out, in0, s1, s2, op0, op1)


COMPUTE_ENGS = ("scalar", "vector", "gpsimd")
PE_ENGS = ("tensor", "sync")


class Builder:
    def __init__(self, debug=False):
        self.nc = bass.Bass("TRN2", target_bir_lowering=False)
        self.debug = debug
        self.ops = {"sync": [], "scalar": [], "vector": [], "tensor": [],
                    "gpsimd": []}
        self.units = []
        self.cur_unit = None
        self.unit_cursor_early = 0
        self.unit_cursor_late = 0
        self.sems = {}
        self._vsb = []
        self._off = 0
        self._ovl_base = None
        self._ovl_max = 0
        self.guard = {}

    def sem(self, name):
        if name not in self.sems:
            self.sems[name] = Sem(name)
        return self.sems[name]

    def _alloc(self, name, shape, dtype, align=64):
        per = int(np.prod(shape[1:])) * mybir.dt.size(dtype)
        off = (self._off + align - 1) // align * align
        self._off = off + per
        h = self.nc.virtual_sbuf_tensor(name, list(shape), dtype)
        self._vsb.append((h, off))
        return h

    def alias_sbuf(self, name, shape, dtype, of_handle):
        off = next(o for h, o in self._vsb if h is of_handle)
        h = self.nc.virtual_sbuf_tensor(name, list(shape), dtype)
        self._vsb.append((h, off))
        return h

    def finalize_arena(self):
        self._ovl_max = max(self._ovl_max, self._off)
        size = (self._ovl_max + 63) // 64 * 64
        assert size <= 212800, f"arena too big: {size} B/partition"
        nc = self.nc
        self._backing_cm = nc.sbuf_tensor("arena", [128, size], U8)
        self._backing = self._backing_cm.__enter__()
        base = nc.lookup_mloc(self._backing).addr
        for h, off in self._vsb:
            nc._tensor(h.name, h.shape, h.dtype, type="SB")
            mloc = nc.lookup_mloc(h)
            mloc.allocated = True
            mloc.addr = base + off
        return size

    def E(self, eng, fn, wait=(), inc=None, defer=False, late=False):
        if inc is not None:
            inc[0].count += inc[1]
        rec = (fn, [w for w in wait if w is not None], inc)
        if defer:
            self.cur_unit["late" if late else "early"][eng].append(rec)
        else:
            self.ops[eng].append(rec)

    def open_unit(self):
        self.cur_unit = {"early": {k: [] for k in self.ops},
                         "late": {k: [] for k in self.ops}, "gev": 0}

    def close_unit(self, gev):
        self.cur_unit["gev"] = gev
        self.units.append(self.cur_unit)
        self.cur_unit = None

    def _emit_late(self, k, state):
        u = self.units[k]
        for eng, lst in u["late"].items():
            for fn, waits, inc in lst:
                if eng == "tensor" and state["first_pe"] \
                        and state["pe_wait"] is not None:
                    waits = list(waits) + [state["pe_wait"]]
                    state["first_pe"] = False
                self.ops[eng].append((fn, waits, inc))
        state["last_gev"] = u["gev"]
        self.unit_cursor_late = k + 1

    def drip_units(self, n, pe_wait=None, ps_guard_cb=None):
        e0, e1 = self.unit_cursor_early, min(self.unit_cursor_early + n,
                                             len(self.units))
        state = {"first_pe": True, "pe_wait": pe_wait, "last_gev": None}
        for k in range(e0, e1):
            for eng, lst in self.units[k]["early"].items():
                self.ops[eng].extend(lst)
            for lk in range(self.unit_cursor_late, k - 1):
                self._emit_late(lk, state)
        self.unit_cursor_early = e1
        if state["last_gev"] is not None and ps_guard_cb is not None:
            ps_guard_cb(state["last_gev"])

    def drip_flush(self, pe_wait=None, ps_guard_cb=None):
        state = {"first_pe": True, "pe_wait": pe_wait, "last_gev": None}
        for k in range(self.unit_cursor_early, len(self.units)):
            for eng, lst in self.units[k]["early"].items():
                self.ops[eng].extend(lst)
            for lk in range(self.unit_cursor_late, k - 1):
                self._emit_late(lk, state)
        self.unit_cursor_early = len(self.units)
        for lk in range(self.unit_cursor_late, len(self.units)):
            self._emit_late(lk, state)
        if state["last_gev"] is not None and ps_guard_cb is not None:
            ps_guard_cb(state["last_gev"])

    def dma(self, eng, out_ap, in_ap, semm, wait=()):
        self.E(eng, lambda e, o=out_ap, i=in_ap: e.dma_start(o, i), wait=wait,
               inc=(semm, 16))
        return semm.count

    def ps_wait(self, name):
        w = self.guard.get(name, [])
        self.guard[name] = []
        return w

    def ps_read(self, name, semm, count):
        self.guard.setdefault(name, []).append((semm, count))


def build_program(debug=False):
    b = Builder(debug)
    nc = b.nc
    E, sem = b.E, b.sem

    x_in = nc.dram_tensor("x", [SEQ_PER_CORE, S, H], F32, kind="ExternalInput").ap()
    w_in = {}
    for nm, shp in (("wq", [H, H]), ("wk", [H, H]), ("wv", [H, H]), ("wo", [H, H]),
                    ("wg", [II, H]), ("wu", [II, H]), ("wd", [H, II])):
        w_in[nm] = nc.dram_tensor(nm, shp, F32, kind="ExternalInput").ap()
    maskS = nc.dram_tensor("maskS", [S, S], BF16, kind="ExternalInput").ap()
    maskF = nc.dram_tensor("maskF", [S, S], BF16, kind="ExternalInput").ap()
    ident_in = nc.dram_tensor("ident", [128, 128], BF16, kind="ExternalInput").ap()
    out_hbm = nc.dram_tensor("out", [SEQ_PER_CORE, S, H], F32,
                             kind="ExternalOutput").ap()
    v_hbm = nc.dram_tensor("vh", [S, NH * 65], BF16, kind="Internal").ap()
    x2s = nc.dram_tensor("x2s", [SEQ_PER_CORE, S, H], F32,
                         kind="ExternalOutput" if debug else "Internal").ap()
    wT = {}
    for nm, shp in (("wqT", [H, H]), ("wkT", [H, H]), ("wvT", [H, H]),
                    ("woT", [H, H]), ("wgT", [H, II]), ("wuT", [H, II]),
                    ("wdT", [II, H])):
        wT[nm] = nc.dram_tensor(nm + "s", shp, BF16, kind="Internal").ap()
    dbg = {}
    if debug:
        for nm, shp, dt_ in (("d_bigT", [128, FC * S], BF16),
                             ("d_atok", [128, TT * H], BF16)):
            dbg[nm] = nc.dram_tensor(nm, shp, dt_, kind="ExternalOutput").ap()

    A = b._alloc
    ident_sb = A("ident_sb", [128, 128], BF16)
    consts = A("consts", [128, 8], F32)
    lnsm = A("lnsm", [128, 16], F32)
    ln2m = A("ln2m", [128, SEQ_PER_CORE * TT], F32)
    ln2r = A("ln2r", [128, SEQ_PER_CORE * TT], F32)
    qs_sm = [[A(f"qs_{nm}{i}", [128, 64], F32)
              for nm in ("bm", "bmp", "inv", "i6", "s2")] for i in range(2)]
    xld = [A(f"xld{i}", [128, H], F32) for i in range(2)]
    x2w = [A(f"x2w{i}", [128, H], F32) for i in range(2)]
    pvr = A("pvr", [128, 16], F32)
    zbf = A("zbf", [128, 272], BF16)

    b._ovl_base = b._off
    g_f = [A(f"g_f{i}", [128, CW], F32) for i in range(6)]
    g_out = [A(f"g_out{i}", [128, CW], BF16) for i in range(2)]
    g_sm = [[A(f"g_{nm}{i}", [128, CW // 16], F32)
             for nm in ("bm", "bmp", "inv", "i6", "s2")] for i in range(2)]
    g_st = A("g_st", [128, CW], F32)
    g_s2f = A("g_s2f", [128, CW], F32)
    g_stbf = b.alias_sbuf("g_stbf", [128, 2 * CW], BF16, g_st)
    g_ev = [A(f"g_ev{i}", [128, 1024], BF16) for i in range(2)]

    bigT = A("bigT", [128, FC * S], BF16)
    atok = A("atok", [128, TT * H], BF16)
    vaug = A("vaug", [128, TT * NH * 65], BF16)
    vsum = A("vsum", [128, NH * 65], BF16)
    qtp = A("qtp", [128, S], BF16)
    ktp = A("ktp", [128, S], BF16)
    wslq = [A(f"wslq{i}", [128, FC * 128], BF16) for i in range(2)]
    wslk = [A(f"wslk{i}", [128, FC * 128], BF16) for i in range(2)]
    wfull = A("wfull", [128, FC * H], BF16)
    ptr_ = [A(f"ptr{i}", [128, QCW], BF16) for i in range(2)]
    ptm = [A(f"ptm{i}", [128, QCW], BF16) for i in range(4)]
    mbuf = [A(f"mbuf{i}", [128, QCW], BF16) for i in range(2)]
    xlnb = [A(f"xlnb{i}", [128, H], BF16) for i in range(2)]
    dum_act = b.alias_sbuf("dum_act", [128, H], F32, ptr_[0])
    p1_end = b._off

    b._ovl_max = max(b._ovl_max, b._off)
    b._off = b._ovl_base
    w2g = A("w2g", [128, FC * II], BF16)
    w2u = A("w2u", [128, FC * II], BF16)
    w2d = A("w2d", [128, ICC * H], BF16)
    CWH = 1024
    p2_f = [A(f"p2_f{i}", [128, CWH], F32) for i in range(6)]
    hh = A("hh", [128, II], F32)
    hq = A("hq", [128, II], BF16)
    hqT = A("hqT", [128, 768], BF16)
    xq_sb = A("xq_sb", [128, H], BF16)
    xqT = A("xqT", [128, H], BF16)
    xln2 = A("xln2", [128, H], F32)
    sg_sb = [A(f"sg_sb{i}", [128, 512], F32) for i in range(2)]

    arena_size = b.finalize_arena()

    psn = {}
    for nm in ("psS00", "psS01", "psS10", "psS11", "psP0", "psP1",
               "psX0", "psX1"):
        psn[nm] = nc.psum_tensor(nm, [128, 512], F32).__enter__()
    psS = [[psn["psS00"], psn["psS01"]], [psn["psS10"], psn["psS11"]]]
    psSn = [["psS00", "psS01"], ["psS10", "psS11"]]
    psP = [psn["psP0"], psn["psP1"]]
    psX0, psX1 = psn["psX0"], psn["psX1"]
    tr_h = nc.virtual_psum_tensor("psTbf", [128, 1024], BF16)
    nc._tensor(tr_h.name, tr_h.shape, tr_h.dtype, type="PSUM")
    _m = nc.lookup_mloc(tr_h)
    _m.allocated = True
    _m.addr = nc.lookup_mloc(psX0).addr
    _m.bank = nc.lookup_mloc(psX0).bank
    psT_bf = tr_h.ap()
    PST = "psX0"
    tr2_h = nc.virtual_psum_tensor("psT2bf", [128, 1024], BF16)
    nc._tensor(tr2_h.name, tr2_h.shape, tr2_h.dtype, type="PSUM")
    _m2 = nc.lookup_mloc(tr2_h)
    _m2.allocated = True
    _m2.addr = nc.lookup_mloc(psn["psP0"]).addr
    _m2.bank = nc.lookup_mloc(psn["psP0"]).bank
    psT2_bf = tr2_h.ap()

    s_qv = sem("qv")
    s_qa = sem("qa")
    s_qp = sem("qp")
    s_gv = sem("gv")
    s_gp = sem("gp")
    s_ga = sem("ga")

    def make_quant(F, SM, s_v, s_p, defer):
        st = {"k": 0, "gp19": []}

        def gen(src_ap, n, out_ap, pre_wait=(), out_wait=()):
            k = st["k"]
            st["k"] += 1
            nb = n // 16
            f = [h.ap()[:, 0:n] for h in F]
            bm, bmp, inv, i6, s2 = (h.ap()[:, 0:nb] for h in SM[k % 2])

            def sv(fn, extra=()):
                E("vector", fn, wait=[(s_v, s_v.count)] + list(extra),
                  inc=(s_v, 1), defer=defer)

            def sp(fn, extra=()):
                E("gpsimd", fn, wait=[(s_p, s_p.count)] + list(extra),
                  inc=(s_p, 1), defer=defer)

            w_sm = [(s_p, st["gp19"][k - 2])] if k >= 2 else []
            sv(lambda e: e.tensor_reduce(
                bm, src_ap.rearrange("p (bb kk) -> p bb kk", kk=16),
                AX.X, ALU.max, apply_absolute_value=True),
               extra=list(pre_wait) + w_sm)
            sv(lambda e: ts_(e, bmp, bm, 6e-12, None, ALU.max))
            sv(lambda e: e.reciprocal(inv, bmp))
            sv(lambda e: ts_(e, i6, inv, 6.0, None, ALU.mult))
            sv(lambda e: ts_(e, s2, bmp, 1.0 / 6.0, None, ALU.mult))
            gv5 = s_v.count
            sp(lambda e, nb_=nb: e.tensor_tensor(
                f[0].rearrange("p (bb kk) -> p kk bb", bb=nb_),
                src_ap.rearrange("p (bb kk) -> p kk bb", bb=nb_),
                i6.rearrange("p (o bb) -> p o bb", o=1)
                .broadcast_to([128, 16, nb_]), ALU.mult),
               extra=[(s_v, gv5)])
            gp6 = s_p.count
            sv(lambda e: ts_(e, f[1], f[0], RC0, RC0, ALU.add, ALU.subtract),
               extra=[(s_p, gp6)])
            sv(lambda e: ts_(e, f[2], f[0], RC1, RC1, ALU.add, ALU.subtract))
            sv(lambda e: ts_(e, f[3], f[0], RC2, RC2, ALU.add, ALU.subtract))
            sv(lambda e: ts_(e, f[0], f[1], -2.0, 2.0, ALU.max, ALU.min))
            sv(lambda e: ts_(e, f[1], f[2], -4.0, 4.0, ALU.max, ALU.min))
            sv(lambda e: ts_(e, f[4], f[2], -2.0, 2.0, ALU.max, ALU.min))
            sv(lambda e: ts_(e, f[2], f[3], -6.0, 6.0, ALU.max, ALU.min))
            sv(lambda e: ts_(e, f[5], f[3], -4.0, 4.0, ALU.max, ALU.min))
            gv14 = s_v.count
            sp(lambda e: e.tensor_tensor(f[3], f[1], f[4], ALU.subtract),
               extra=[(s_v, gv14)])
            sp(lambda e: e.tensor_tensor(f[1], f[2], f[5], ALU.subtract))
            sp(lambda e: e.tensor_tensor(f[4], f[3], f[1], ALU.add))
            sp(lambda e: e.tensor_tensor(f[2], f[4], f[0], ALU.add))
            sp(lambda e, nb_=nb: e.tensor_tensor(
                out_ap.rearrange("p (bb kk) -> p kk bb", bb=nb_),
                f[2].rearrange("p (bb kk) -> p kk bb", bb=nb_),
                s2.rearrange("p (o bb) -> p o bb", o=1)
                .broadcast_to([128, 16, nb_]), ALU.mult),
               extra=list(out_wait))
            st["gp19"].append(s_p.count)
            return s_p.count, gp6
        return gen


    def make_quant_dve(F, SM, s_v, s_a, S2F=None, defer=False):
        st = {"k": 0, "v_cnt": [], "out_cnt": [], "t3": 0, "S": 0,
              "out": 0, "a_cnt": []}

        def sv(fn, extra=(), chain=True):
            E("vector", fn,
              wait=([(s_v, s_v.count)] if chain else []) + list(extra),
              inc=(s_v, 1), defer=defer)

        def sa(fn, extra=()):
            E("scalar", fn, wait=list(extra), inc=(s_a, 1), defer=defer)

        def gen(src_ap, n, out_ap, pre_wait=(), out_wait=()):
            k = st["k"]
            st["k"] += 1
            nb = n // 16
            f = [h.ap()[:, 0:n] for h in F]
            bm, bmp, inv, i6, s2 = (h.ap()[:, 0:nb] for h in SM[k % 2])
            w_sm = ([(s_v, st["out_cnt"][k - 2]),
                     (s_a, st["a_cnt"][k - 2])] if k >= 2 else [])
            sv(lambda e: e.tensor_reduce(
                bm, src_ap.rearrange("p (bb kk) -> p bb kk", kk=16),
                AX.X, ALU.max, apply_absolute_value=True),
               extra=list(pre_wait) + w_sm)
            sv(lambda e: ts_(e, bmp, bm, 6e-12, None, ALU.max))
            sv(lambda e: e.reciprocal(inv, bmp))
            sv(lambda e: ts_(e, i6, inv, 6.0, None, ALU.mult))
            sv(lambda e: ts_(e, s2, bmp, 1.0 / 6.0, None, ALU.mult))
            gv5 = s_v.count
            sa(lambda e, nb_=nb: e.activation(
                f[0].rearrange("p (bb kk) -> p kk bb", bb=nb_),
                i6.rearrange("p (o bb) -> p o bb", o=1)
                .broadcast_to([128, 16, nb_]), AF.Copy),
               extra=list(pre_wait) + [(s_v, max(gv5, st["t3"]))])
            ca1 = s_a.count
            if S2F is not None:
                s2f = S2F.ap()[:, 0:n]
                sa(lambda e, nb_=nb: e.activation(
                    s2f.rearrange("p (bb kk) -> p kk bb", bb=nb_),
                    s2.rearrange("p (o bb) -> p o bb", o=1)
                    .broadcast_to([128, 16, nb_]), AF.Copy),
                   extra=[(s_v, max(gv5, st["out"]))])
            ca2 = s_a.count
            sv(lambda e: e.tensor_tensor(f[1], src_ap, f[0], ALU.mult),
               extra=[(s_a, ca1), (s_v, st["S"])], chain=False)
            cv = s_v.count
            sv(lambda e: ts_(e, f[2], f[1], RC0, RC0, ALU.add, ALU.subtract),
               extra=[(s_v, cv)], chain=False)
            cw0 = s_v.count
            sv(lambda e: ts_(e, f[3], f[1], RC1, RC1, ALU.add, ALU.subtract),
               extra=[(s_v, cv)], chain=False)
            cw1 = s_v.count
            sv(lambda e: ts_(e, f[4], f[1], RC2, RC2, ALU.add, ALU.subtract),
               extra=[(s_v, cv)], chain=False)
            cw2 = s_v.count
            sv(lambda e: ts_(e, f[5], f[2], -2.0, 2.0, ALU.max, ALU.min),
               extra=[(s_v, cw0)], chain=False)
            cc0 = s_v.count
            sv(lambda e: ts_(e, f[0], f[3], -4.0, 4.0, ALU.max, ALU.min),
               extra=[(s_v, max(cw1, cv))], chain=False)
            cA1 = s_v.count
            sv(lambda e: ts_(e, f[1], f[3], -2.0, 2.0, ALU.max, ALU.min),
               extra=[(s_v, cw2)], chain=False)
            cB1 = s_v.count
            sv(lambda e: ts_(e, f[2], f[4], -6.0, 6.0, ALU.max, ALU.min),
               extra=[(s_v, cc0)], chain=False)
            cA2 = s_v.count
            sv(lambda e: ts_(e, f[3], f[4], -4.0, 4.0, ALU.max, ALU.min),
               extra=[(s_v, cB1)], chain=False)
            cB2 = s_v.count
            sv(lambda e: e.tensor_tensor(f[4], f[0], f[1], ALU.subtract),
               extra=[(s_v, cB2)], chain=False)
            cu = s_v.count
            sv(lambda e: e.tensor_tensor(f[0], f[2], f[3], ALU.subtract),
               extra=[(s_v, cu)], chain=False)
            ct2 = st["t3"] = s_v.count
            sv(lambda e: e.tensor_tensor(f[1], f[4], f[0], ALU.add),
               extra=[(s_v, ct2)], chain=False)
            ct3 = s_v.count
            sv(lambda e: e.tensor_tensor(f[2], f[1], f[5], ALU.add),
               extra=[(s_v, ct3)], chain=False)
            cS = st["S"] = s_v.count
            if S2F is not None:
                sv(lambda e: e.tensor_tensor(out_ap, f[2], s2f, ALU.mult),
                   extra=[(s_v, cS), (s_a, ca2)] + list(out_wait),
                   chain=False)
            else:
                sv(lambda e, nb_=nb: e.tensor_tensor(
                    out_ap.rearrange("p (bb kk) -> p kk bb", bb=nb_),
                    f[2].rearrange("p (bb kk) -> p kk bb", bb=nb_),
                    s2.rearrange("p (o bb) -> p o bb", o=1)
                    .broadcast_to([128, 16, nb_]), ALU.mult),
                   extra=[(s_v, cS)] + list(out_wait), chain=False)
            st["out"] = s_v.count
            st["v_cnt"].append(cv)
            st["out_cnt"].append(s_v.count)
            st["a_cnt"].append(ca2)
            return k, cv
        return gen, st

    gen_quant_dve, q2_st = make_quant_dve(p2_f, qs_sm, s_qv, s_qa)
    gen_quant_drip, gd_st = make_quant_dve(g_f, g_sm, s_gv, s_ga,
                                           S2F=g_s2f, defer=True)


    def q2_flush():
        pass

    s_wld = sem("wld")
    s_p0mm = sem("p0mm")
    s_p0ev = sem("p0ev")
    s_p0st = [sem("p0st0"), sem("p0st1")]
    s_ini = sem("ini")

    E("vector", lambda e: e.memset(consts.ap()[:, 0:1], 1e-5), inc=(s_ini, 1))
    E("vector", lambda e: e.memset(consts.ap()[:, 1:2], -EXPB),
      wait=[(s_ini, s_ini.count)], inc=(s_ini, 1))
    E("vector", lambda e: e.memset(zbf.ap(), 0.0),
      wait=[(s_ini, s_ini.count)], inc=(s_ini, 1))
    INI = s_ini.count
    s_idl = sem("idl")
    b.dma("sync", ident_sb.ap(), ident_in, s_idl)
    IDL = s_idl.count

    p0ev_attn = [A if False else None]
    for nm in ("wq", "wk", "wv", "wo"):
        src, dst = w_in[nm], wT[nm + "T"]
        for ot in range(FC):
            st_bf = g_stbf.ap()[:, 0:1024]
            E("gpsimd", lambda e, o=st_bf, i=src[ot * 128:(ot + 1) * 128, :]:
              e.dma_start(o, i), wait=[(s_p0mm, s_p0mm.count)], inc=(s_wld, 16))
            ld = s_wld.count
            for k in range(FC):
                E("tensor", lambda e, k=k, src_=st_bf: e.transpose(
                    psT_bf[:, k * 128:(k + 1) * 128],
                    src_[:, k * 128:(k + 1) * 128], ident_sb.ap()),
                  wait=([(s_wld, ld), (s_idl, IDL)] + b.ps_wait(PST))
                  if k == 0 else [],
                  inc=((s_p0mm, 1) if k == FC - 1 else None))
            mm = s_p0mm.count
            ev = g_ev[ot % 2]
            E("vector", lambda e, ev=ev: e.tensor_copy(
                ev.ap()[:, 0:1024], psT_bf[:, 0:1024]),
              wait=[(s_p0mm, mm), (s_p0st[ot % 2], s_p0st[ot % 2].count)],
              inc=(s_p0ev, 1))
            b.ps_read(PST, s_p0ev, s_p0ev.count)
            evc = s_p0ev.count
            dst_ap = bass.AP(dst.tensor, ot * 128,
                             [[H, 128], [128 * H, FC], [1, 128]])
            E("sync", lambda e, o=dst_ap, i=ev.ap()[:, 0:1024].rearrange(
                "p (k c) -> p k c", k=FC): e.dma_start(o, i),
              wait=[(s_p0ev, evc)], inc=(s_p0st[ot % 2], 16))
    P0A_ST0 = s_p0st[0].count
    P0A_ST1 = s_p0st[1].count
    P0A_W = [(s_p0st[0], P0A_ST0), (s_p0st[1], P0A_ST1)]

    s_gld = sem("gld")
    s_gmm = sem("gmm")
    s_gev = sem("gev")
    s_gst = [sem("gst0"), sem("gst1")]

    unit_no = [0]
    gp6_prev = [0]
    for nm, rows, cols in (("wg", II, H), ("wu", II, H), ("wd", H, II)):
        src, dst = w_in[nm], wT[nm + "T"]
        for ot in range(rows // 128):
            for cc0 in range(0, cols, CW):
                b.open_unit()
                unit_no[0] += 1
                cw = min(CW, cols - cc0)
                E("scalar", lambda e, o=g_st.ap()[:, 0:cw],
                  i=src[ot * 128:(ot + 1) * 128, cc0:cc0 + cw]:
                  e.dma_start(o, i),
                  wait=[(s_gv, gp6_prev[0])], inc=(s_gld, 16), defer=True)
                ld = s_gld.count
                qdst = g_out[unit_no[0] % 2].ap()[:, 0:cw]
                kq, gp6_prev[0] = gen_quant_drip(
                    g_st.ap()[:, 0:cw], cw, qdst,
                    pre_wait=[(s_gld, ld)],
                    out_wait=[(s_gmm, max(0, s_gmm.count - 1))])
                gqn = gd_st["out_cnt"][kq]
                nck = cw // 128
                for k in range(nck):
                    E("tensor", lambda e, k=k, qdst=qdst: e.transpose(
                        psT2_bf[:, k * 128:(k + 1) * 128],
                        qdst[:, k * 128:(k + 1) * 128], ident_sb.ap()),
                      wait=[(s_gv, gqn), (s_gev, s_gev.count)]
                      if k == 0 else [],
                      inc=((s_gmm, 1) if k == nck - 1 else None),
                      defer=True, late=True)
                mm = s_gmm.count
                ev = g_ev[unit_no[0] % 2]
                E("vector", lambda e, nck=nck, ev=ev: e.tensor_copy(
                    ev.ap()[:, 0:nck * 128], psT2_bf[:, 0:nck * 128]),
                  wait=[(s_gmm, mm),
                        (s_gst[unit_no[0] % 2], s_gst[unit_no[0] % 2].count)],
                  inc=(s_gev, 1), defer=True, late=True)
                evc = s_gev.count
                dst_ap = bass.AP(dst.tensor, cc0 * rows + ot * 128,
                                 [[rows, 128], [128 * rows, nck], [1, 128]])
                E("sync", lambda e, o=dst_ap,
                  i=ev.ap()[:, 0:nck * 128].rearrange("p (k c) -> p k c",
                                                      k=nck):
                  e.dma_start(o, i),
                  wait=[(s_gev, evc)], inc=(s_gst[unit_no[0] % 2], 16),
                  defer=True, late=True)
                b.close_unit(evc)

    import os as _os
    if _os.environ.get("K_NODRIP"):
        b.drip_units(len(b.units))
        b.drip_units(len(b.units))
    P0B_ST0 = s_gst[0].count
    P0B_ST1 = s_gst[1].count
    P0B_GP = s_gp.count
    P0B_GV = s_gv.count
    P0B_GEV = s_gev.count
    P0B_GMM = s_gmm.count

    s_x = [sem("x0"), sem("x1")]
    s_ln = sem("ln")
    s_lnv = sem("lnv")
    s_ap1 = sem("ap1")
    s_xt = sem("xt")
    s_xte = sem("xte")
    s_wf = sem("wf")
    s_vps = sem("vps")
    s_vev = sem("vev")
    s_qkl = [sem("qkl0"), sem("qkl1")]
    s_qkm = sem("qkm")
    s_qke = sem("qke")
    s_sa = sem("sa")
    s_sb = sem("sb")
    s_ea = sem("ea")
    s_eb = sem("eb")
    s_ma = sem("ma")
    s_mb = sem("mb")
    s_pva = sem("pva")
    s_pvb = sem("pvb")
    s_pvn = sem("pvn")
    s_mld = [sem("mld0"), sem("mld1")]
    s_wo = sem("wo")
    s_res = sem("res")
    s_x2w = [sem("x2w0"), sem("x2w1")]

    def ln_stats(xap, m_slot, r_slot, pre_wait):
        su = lnsm.ap()[:, 0:1]
        sq = lnsm.ap()[:, 1:2]
        var = lnsm.ap()[:, 3:4]
        lnv = lnsm.ap()[:, 4:5]
        E("scalar", lambda e: e.activation(dum_act.ap(), xap, AF.Copy,
                                           accum_out=su),
          wait=list(pre_wait) + [(s_lnv, s_lnv.count)], inc=(s_ln, 1))
        E("scalar", lambda e: e.activation(dum_act.ap(), xap, AF.Square,
                                           accum_out=sq),
          wait=[(s_ln, s_ln.count)], inc=(s_ln, 1))
        lc = s_ln.count
        E("vector", lambda e, m=m_slot: ts_(e, m, su, 1.0 / H, None, ALU.mult),
          wait=[(s_ln, lc)], inc=(s_lnv, 1))
        E("vector", lambda e, m=m_slot: e.tensor_tensor(var, m, m, ALU.mult),
          wait=[(s_lnv, s_lnv.count)], inc=(s_lnv, 1))
        E("vector", lambda e: e.scalar_tensor_tensor(var, sq, 1.0 / H, var,
                                                     ALU.mult, ALU.subtract),
          wait=[(s_lnv, s_lnv.count)], inc=(s_lnv, 1))
        vc = s_lnv.count
        E("scalar", lambda e: e.activation(lnv, var, AF.Ln,
                                           bias=consts.ap()[:, 0:1]),
          wait=[(s_lnv, vc), (s_ini, INI)], inc=(s_ln, 1))
        E("scalar", lambda e, r=r_slot: e.activation(r, lnv, AF.Exp,
                                                     scale=-0.5),
          wait=[(s_ln, s_ln.count)], inc=(s_ln, 1))
        return s_ln.count

    def emit_pv(hb, hg, tkind, tv, first, qlo, ktmax, pt_m, wsem, wcnt,
                pv0):
        s_pv = s_pva if hb == 0 else s_pvb
        psPn = "psP%d" % hb
        kt = tv if tkind == "band" else 0
        first_q = max(kt, qlo // 128) if tkind == "band" else qlo // 128
        for qg in range(first_q, ktmax):
            ql = qg - qlo // 128
            slot = ql * 65
            wpv = []
            if qg == first_q:
                wpv = [(wsem, wcnt), (s_vev, VEV[0])]
                if tkind == "sum":
                    wpv.append((s_vsm, VSM[0]))
                if first:
                    wpv += b.ps_wait(psPn)
            if tkind == "band":
                rhs = vaug.ap()[:, kt * NH * 65 + hg * 65:
                                kt * NH * 65 + hg * 65 + 65]
                lhs_np = 128
            else:
                rhs = vsum.ap()[0:tv, hg * 65:hg * 65 + 65]
                lhs_np = tv
            E("tensor", lambda e, pm=pt_m, qg=qg, qlo=qlo, rhs=rhs,
              hb=hb, slot=slot, ql=ql, lhs_np=lhs_np, kt=kt, first=first,
              tkind=tkind: e.matmul(
                psP[hb][:, slot:slot + 65],
                pm.ap()[0:lhs_np, qg * 128 - qlo:qg * 128 - qlo + 128],
                rhs,
                start=(first and ql == 0),
                stop=(tkind == "band" and kt == qg), skip_group_check=True),
              wait=wpv,
              inc=((s_pv, 1) if qg == ktmax - 1 else None))

    NPAIR = NH // 2
    mask_hist = []
    VEV = [0]
    VSM = [0]
    s_vhb = sem("vhb")
    s_vsm = sem("vsm")

    for sqi in range(SEQ_PER_CORE):
        for tt in range(TT):
            bufi = tt % 2
            wpre = [(s_ap1, max(0, s_ap1.count - 1)), (s_res, s_res.count)]
            if sqi == 0 and tt == 0:
                wpre.extend(P0A_W)
            E("sync", lambda e, o=xld[bufi].ap(),
              i=x_in[sqi, tt * 128:(tt + 1) * 128, :]: e.dma_start(o, i),
              wait=wpre, inc=(s_x[bufi], 16))
            xc = s_x[bufi].count
            xap = xld[bufi].ap()
            lc = ln_stats(xap, lnsm.ap()[:, 2:3], lnsm.ap()[:, 7:8],
                          [(s_x[bufi], xc)])
            xb_ = xlnb[bufi].ap()
            E("vector", lambda e, xa=xap, o=xb_: ts_(
                e, o, xa, lnsm.ap()[:, 2:3], lnsm.ap()[:, 7:8],
                ALU.subtract, ALU.mult),
              wait=[(s_ln, lc), (s_xt, max(0, s_xt.count - 1))], inc=(s_ap1, 1))
            ap_c = s_ap1.count
            for k in range(FC):
                E("tensor", lambda e, k=k, i=xb_[:, k * 128:(k + 1) * 128]:
                  e.transpose(psT_bf[:, k * 128:(k + 1) * 128], i,
                              ident_sb.ap()),
                  wait=([(s_ap1, ap_c)] + b.ps_wait(PST)) if k == 0 else [],
                  inc=((s_xt, 1) if k == FC - 1 else None))
            mmc = s_xt.count
            E("vector", lambda e, tt=tt: e.tensor_copy(
                bigT.ap().rearrange("p (c t) -> p c t", c=FC)
                [:, :, tt * 128:(tt + 1) * 128],
                psT_bf[:, 0:1024].rearrange("p (c t) -> p c t", c=FC)),
              wait=[(s_xt, mmc)], inc=(s_xte, 1))
            b.ps_read(PST, s_xte, s_xte.count)
        XTE = s_xte.count
        b.drip_units(2, pe_wait=(s_pvn, s_pvn.count),
                     ps_guard_cb=lambda g: b.ps_read("psP0", s_gev, g))

        E("scalar", lambda e: e.dma_start(
            wfull.ap().rearrange("p (c n) -> p c n", c=FC),
            wT["wvT"].rearrange("(c p) n -> p c n", p=128)),
          wait=P0A_W + [(s_qkm, s_qkm.count), (s_wo, s_wo.count)],
          inc=(s_wf, 16))
        wfc = s_wf.count
        E("vector", lambda e: e.memset(
            vaug.ap().rearrange("p (t h j) -> p t h j", h=NH, j=65)
            [:, :, :, 64:65], 1.0),
          wait=[(s_pvn, s_pvn.count)], inc=(s_vev, 1))
        for tt in range(TT):
            for nch in range(2):
                psV, psVn = (psX1, "psX1") if nch == 0 else (psn["psS00"],
                                                             "psS00")
                for fc in range(FC):
                    E("tensor", lambda e, tt=tt, fc=fc, nch=nch, psV=psV:
                      e.matmul(
                        psV[:, :],
                        bigT.ap().rearrange("p (c t) -> p c t", c=FC)
                        [:, fc, tt * 128:(tt + 1) * 128],
                        wfull.ap().rearrange("p (c n) -> p c n", c=FC)
                        [:, fc, nch * 512:(nch + 1) * 512],
                        start=(fc == 0), stop=(fc == FC - 1), skip_group_check=True),
                      wait=([(s_wf, wfc), (s_xte, XTE)] + b.ps_wait(psVn))
                      if fc == 0 else [],
                      inc=((s_vps, 1) if fc == FC - 1 else None))
                vp = s_vps.count
                E("vector", lambda e, tt=tt, nch=nch, psV=psV: e.tensor_copy(
                    vaug.ap().rearrange("p (t h j) -> p t h j", h=NH, j=65)
                    [:, tt, nch * 8:(nch + 1) * 8, 0:64],
                    psV[:, :].rearrange("p (h j) -> p h j", h=8)),
                  wait=[(s_vps, vp)], inc=(s_vev, 1))
                b.ps_read(psVn, s_vev, s_vev.count)
            vevc = s_vev.count
            E("sync", lambda e, tt=tt: e.dma_start(
                v_hbm[tt * 128:(tt + 1) * 128, :],
                vaug.ap()[:, tt * NH * 65:(tt + 1) * NH * 65]),
              wait=[(s_vev, vevc)] + ([(s_vsm, VSM[0])] if tt == 0 else []),
              inc=(s_vhb, 16))
        VEV[0] = s_vev.count
        E("sync", lambda e: e.dma_start(
            vsum.ap()[0:45, :],
            bass.AP(v_hbm.tensor, 44 * NH * 65,
                    [[45 * NH * 65, 45], [1, NH * 65]])),
          wait=[(s_vhb, s_vhb.count)], inc=(s_vsm, 16))
        VSM[0] = s_vsm.count

        for pair in range(NPAIR):
            mask_t = maskS if pair < NPAIR // 2 else maskF
            pc = pair * 128
            wbq, wbk = wslq[pair % 2], wslk[pair % 2]
            wq_ap = bass.AP(wT["wqT"].tensor, pc, [[H, 128], [128 * H, FC],
                                                   [1, 128]])
            wk_ap = bass.AP(wT["wkT"].tensor, pc, [[H, 128], [128 * H, FC],
                                                   [1, 128]])
            wpre = ([(s_qkm, max(0, s_qkm.count - 16))]
                    if (sqi * NPAIR + pair) >= 2 else list(P0A_W))
            s_qkl_p = s_qkl[pair % 2]
            E("scalar", lambda e, o=wbq.ap().rearrange("p (c n) -> p c n",
                                                       c=FC),
              i=wq_ap: e.dma_start(o, i), wait=wpre, inc=(s_qkl_p, 16))
            E("scalar", lambda e, o=wbk.ap().rearrange("p (c n) -> p c n",
                                                       c=FC),
              i=wk_ap: e.dma_start(o, i), wait=[], inc=(s_qkl_p, 16))
            qkc = s_qkl_p.count
            for (wb, outb) in ((wbq, qtp), (wbk, ktp)):
                for nchk in range(4):
                    psQ, psQn = ((psX1, "psX1") if nchk % 2 == 0
                                 else (psn["psS00"], "psS00"))
                    for fc in range(FC):
                        E("tensor", lambda e, wb=wb, nchk=nchk, fc=fc, psQ=psQ:
                          e.matmul(
                            psQ[:, :],
                            wb.ap().rearrange("p (c n) -> p c n", c=FC)
                            [:, fc, :],
                            bigT.ap().rearrange("p (c t) -> p c t", c=FC)
                            [:, fc, nchk * 512:(nchk + 1) * 512],
                            start=(fc == 0), stop=(fc == FC - 1), skip_group_check=True),
                          wait=([(s_qkl_p, qkc), (s_xte, XTE)] + b.ps_wait(psQn))
                          if fc == 0 else [],
                          inc=((s_qkm, 1) if fc == FC - 1 else None))
                    qm = s_qkm.count
                    E("vector", lambda e, outb=outb, nchk=nchk, psQ=psQ:
                      e.tensor_copy(outb.ap()[:, nchk * 512:(nchk + 1) * 512],
                                    psQ[:, :]),
                      wait=[(s_qkm, qm)], inc=(s_qke, 1))
                    b.ps_read(psQn, s_qke, s_qke.count)
            QKE = s_qke.count

            is_fixed = pair >= NPAIR // 2
            for qc in range(NQC):
                qlo, qhi = qc * QCW, qc * QCW + QCW
                ktmax = qhi // 128
                ktlo = max(0, (qlo - 44) // 128) if is_fixed else 0
                Mc = 0
                if is_fixed and ktlo > 0:
                    Mc = (ktlo * 128 - 45) // 45 + 1
                tiles = ([("sum", Mc)] if Mc > 0 else []) \
                    + [("band", kt) for kt in range(ktlo, ktmax)]
                pva0, pvb0 = s_pva.count, s_pvb.count
                prev = [None, None]
                for idx, (tkind, tv) in enumerate(tiles):
                    if tkind == "band":
                        kt = tv
                        qstart = max(qlo, kt * 128)
                        n = qhi - qstart
                        c0 = qstart - qlo
                        g = len(mask_hist)
                        mb_ = mbuf[g % 2]
                        wm = []
                        if g >= 2:
                            wm = [(s_ma, mask_hist[g - 2][0]),
                                  (s_mb, mask_hist[g - 2][1])]
                        E("sync", lambda e, o=mb_.ap()[:, 0:n],
                          i=mask_t[kt * 128:(kt + 1) * 128, qstart:qhi]:
                          e.dma_start(o, i), wait=wm, inc=(s_mld[g % 2], 16))
                        mldc = s_mld[g % 2].count
                        s_mld_g = s_mld[g % 2]
                        mask_hist.append(None)
                    for hb in (0, 1):
                        hg = pair * 2 + hb
                        psSh = psS[hb][idx % 2]
                        psShn = psSn[hb][idx % 2]
                        s_s = s_sa if hb == 0 else s_sb
                        s_e = s_ea if hb == 0 else s_eb
                        s_m = s_ma if hb == 0 else s_mb
                        s_pv = s_pva if hb == 0 else s_pvb
                        pv0 = pva0 if hb == 0 else pvb0
                        pt_raw = ptr_[hb]
                        pt_m = ptm[2 * hb + (idx % 2)]
                        if tkind == "band":
                            E("tensor", lambda e, kt=kt, hb=hb, c0=c0, n=n,
                              qs=qstart, psSh=psSh: e.matmul(
                                psSh[:, c0:c0 + n],
                                ktp.ap()[64 * hb:64 * hb + 64,
                                         kt * 128:(kt + 1) * 128],
                                qtp.ap()[64 * hb:64 * hb + 64, qs:qs + n],
                                start=True, stop=True, skip_group_check=True),
                              wait=[(s_qke, QKE)] + b.ps_wait(psShn),
                              inc=(s_s, 1))
                        else:
                            sum_ap = bass.AP(ktp.ap().tensor,
                                             (64 * hb) * S + 44,
                                             [[S, 64], [45, Mc]])
                            E("tensor", lambda e, hb=hb, psSh=psSh,
                              sum_ap=sum_ap, qlo=qlo, qhi=qhi, Mc=Mc:
                              e.matmul(
                                psSh[0:Mc, 0:QCW],
                                sum_ap,
                                qtp.ap()[64 * hb:64 * hb + 64, qlo:qhi],
                                start=True, stop=True, skip_group_check=True),
                              wait=[(s_qke, QKE)] + b.ps_wait(psShn),
                              inc=(s_s, 1))
                        sc = s_s.count
                        if prev[hb] is not None:
                            emit_pv(*prev[hb])
                        if tkind == "band":
                            E("scalar", lambda e, psSh=psSh, c0=c0, n=n,
                              pt=pt_raw: e.activation(
                                pt.ap()[:, c0:c0 + n], psSh[:, c0:c0 + n],
                                AF.Exp,
                                bias=consts.ap()[:, 1:2], scale=SCALE),
                              wait=[(s_s, sc), (s_m, s_m.count),
                                    (s_ini, INI)],
                              inc=(s_e, 1))
                            b.ps_read(psShn, s_e, s_e.count)
                            ec = s_e.count
                            wmm = [(s_e, ec), (s_mld_g, mldc)]
                            if idx >= 2:
                                wmm.append((s_pv, pv0 + idx - 1))
                            E("vector", lambda e, pt=pt_raw, pm=pt_m,
                              mb2=mb_, c0=c0, n=n: e.tensor_tensor(
                                pm.ap()[:, c0:c0 + n], pt.ap()[:, c0:c0 + n],
                                mb2.ap()[:, 0:n], ALU.mult),
                              wait=wmm, inc=(s_m, 1))
                            mc = s_m.count
                            prev[hb] = (hb, hg, "band", kt, idx == 0, qlo,
                                        ktmax, pt_m, s_m, mc, pv0)
                            mask_hist[g] = (s_ma.count, s_mb.count)
                        else:
                            drain = sc + 1
                            E("scalar", lambda e, psSh=psSh, pm=pt_m,
                              Mc=Mc: e.activation(
                                pm.ap()[0:Mc, 0:QCW], psSh[0:Mc, 0:QCW],
                                AF.Exp,
                                bias=consts.ap()[0:Mc, 1:2], scale=SCALE),
                              wait=[(s_s, drain), (s_m, s_m.count),
                                    (s_ini, INI)],
                              inc=(s_e, 1))
                            b.ps_read(psShn, s_e, s_e.count)
                            ec = s_e.count
                            prev[hb] = (hb, hg, "sum", Mc, idx == 0, qlo,
                                        ktmax, pt_m, s_e, ec, pv0)
                for hb in (0, 1):
                    if prev[hb] is not None:
                        emit_pv(*prev[hb])
                        prev[hb] = None
                for hb, s_pv in ((0, s_pva), (1, s_pvb)):
                    E("tensor", lambda e, hb=hb: e.matmul(
                        psP[hb][:, 0:260], zbf.ap()[:, 0:128],
                        zbf.ap()[:, 0:260], start=False, stop=True, skip_group_check=True),
                      wait=[(s_ini, INI)], inc=(s_pv, 1))
                for hb in (0, 1):
                    hg = pair * 2 + hb
                    s_pv = s_pva if hb == 0 else s_pvb
                    pvc = s_pv.count
                    for ql in range(QCW // 128):
                        qg = qlo // 128 + ql
                        slot = ql * 65
                        rr = pvr.ap()[:, hb * 8 + ql:hb * 8 + ql + 1]
                        w1_ = [(s_pvn, s_pvn.count)]
                        if ql == 0:
                            w1_.append((s_pv, pvc))
                        E("vector", lambda e, hb=hb, slot=slot, rr=rr:
                          e.reciprocal(rr, psP[hb][:, slot + 64:slot + 65]),
                          wait=w1_, inc=(s_pvn, 1))
                        E("vector", lambda e, hb=hb, slot=slot, rr=rr, qg=qg,
                          hg=hg: ts_(
                            e, atok.ap().rearrange("p (t c) -> p t c", t=TT)
                            [:, qg, hg * 64:(hg + 1) * 64],
                            psP[hb][:, slot:slot + 64], rr, None, ALU.mult),
                          wait=[(s_pvn, s_pvn.count)], inc=(s_pvn, 1))
                        b.ps_read("psP%d" % hb, s_pvn, s_pvn.count)
            b.drip_units(4, pe_wait=(s_pvn, s_pvn.count),
                         ps_guard_cb=lambda g: b.ps_read("psP0", s_gev, g))
        ATOK = s_pvn.count

        for tt in range(TT):
            for k in range(FC):
                E("tensor", lambda e, tt=tt, k=k: e.transpose(
                    psT_bf[:, k * 128:(k + 1) * 128],
                    atok.ap()[:, tt * 1024 + k * 128:tt * 1024 + (k + 1) * 128],
                    ident_sb.ap()),
                  wait=([(s_pvn, ATOK)] + b.ps_wait(PST)) if k == 0 else [],
                  inc=((s_xt, 1) if k == FC - 1 else None))
            atc = s_xt.count
            E("vector", lambda e, tt=tt: e.tensor_copy(
                bigT.ap().rearrange("p (c t) -> p c t", c=FC)
                [:, :, tt * 128:(tt + 1) * 128],
                psT_bf[:, 0:1024].rearrange("p (c t) -> p c t", c=FC)),
              wait=[(s_xt, atc)], inc=(s_xte, 1))
            b.ps_read(PST, s_xte, s_xte.count)
        ATE = s_xte.count

        E("scalar", lambda e: e.dma_start(
            wfull.ap().rearrange("p (c n) -> p c n", c=FC),
            wT["woT"].rearrange("(c p) n -> p c n", p=128)),
          wait=[(s_vps, s_vps.count), (s_qkm, s_qkm.count)], inc=(s_wf, 16))
        wfc = s_wf.count
        for tt in range(TT):
            bufi = tt % 2
            E("sync", lambda e, o=xld[bufi].ap(),
              i=x_in[sqi, tt * 128:(tt + 1) * 128, :]: e.dma_start(o, i),
              wait=[(s_ap1, s_ap1.count), (s_res, max(0, s_res.count - 2))],
              inc=(s_x[bufi], 16))
            xc = s_x[bufi].count
            for ncb in range(2):
                psW, psWn = ((psX1, "psX1") if ncb == 0
                             else (psn["psS00"], "psS00"))
                for fc in range(FC):
                    E("tensor", lambda e, tt=tt, fc=fc, ncb=ncb, psW=psW:
                      e.matmul(
                        psW[:, :],
                        bigT.ap().rearrange("p (c t) -> p c t", c=FC)
                        [:, fc, tt * 128:(tt + 1) * 128],
                        wfull.ap().rearrange("p (c n) -> p c n", c=FC)
                        [:, fc, ncb * 512:(ncb + 1) * 512],
                        start=(fc == 0), stop=(fc == FC - 1), skip_group_check=True),
                      wait=([(s_wf, wfc), (s_xte, ATE)] + b.ps_wait(psWn))
                      if fc == 0 else [],
                      inc=((s_wo, 1) if fc == FC - 1 else None))
            woc = s_wo.count
            x2b = x2w[bufi]
            E("vector", lambda e, x2b=x2b, bufi=bufi: e.tensor_tensor(
                x2b.ap()[:, 0:512], psX1[:, :], xld[bufi].ap()[:, 0:512],
                ALU.add),
              wait=[(s_wo, woc), (s_x[bufi], xc),
                    (s_x2w[bufi], s_x2w[bufi].count)],
              inc=(s_res, 1))
            E("vector", lambda e, x2b=x2b, bufi=bufi: e.tensor_tensor(
                x2b.ap()[:, 512:1024], psn["psS00"][:, :],
                xld[bufi].ap()[:, 512:1024], ALU.add),
              wait=[(s_res, s_res.count)], inc=(s_res, 1))
            rc = s_res.count
            b.ps_read("psX1", s_res, rc)
            b.ps_read("psS00", s_res, rc)
            gt = sqi * TT + tt
            lc = ln_stats(x2b.ap(), ln2m.ap()[:, gt:gt + 1],
                          ln2r.ap()[:, gt:gt + 1], [(s_res, rc)])
            E("sync", lambda e, x2b=x2b, sqi=sqi, tt=tt: e.dma_start(
                x2s[sqi, tt * 128:(tt + 1) * 128, :], x2b.ap()),
              wait=[(s_ln, lc)], inc=(s_x2w[bufi], 16))
        b.drip_units(3, pe_wait=(s_pvn, s_pvn.count),
                     ps_guard_cb=lambda g: b.ps_read("psP0", s_gev, g))
        if debug and sqi == 0:
            b.dma("sync", dbg["d_bigT"], bigT.ap(), sem("dbgs"),
                  wait=[(s_xte, ATE)])
            b.dma("sync", dbg["d_atok"], atok.ap(), sem("dbgs"),
                  wait=[(s_pvn, ATOK)])

    P1_X2W0 = s_x2w[0].count
    P1_X2W1 = s_x2w[1].count
    X2W_W = [(s_x2w[0], P1_X2W0), (s_x2w[1], P1_X2W1)]
    P1_LN = s_ln.count
    b.drip_flush(pe_wait=(s_pvn, s_pvn.count),
                 ps_guard_cb=lambda g: b.ps_read("psP0", s_gev, g))

    s_w2 = sem("w2")
    s_x2r = [sem("x2r0"), sem("x2r1")]
    s_t2 = sem("t2")
    s_te2 = sem("te2")
    s_gu = sem("gu")
    s_sil = sem("sil")
    s_hm = sem("hm")
    s_dn = sem("dn")
    s_ov = sem("ov")
    s_ow = [sem("ow0"), sem("ow1")]
    s_ap2 = sem("ap2")

    E("scalar", lambda e: e.dma_start(
        w2g.ap().rearrange("p (c n) -> p c n", c=FC),
        wT["wgT"].rearrange("(c p) n -> p c n", p=128)),
      wait=X2W_W + [(s_qkm, s_qkm.count), (s_wo, s_wo.count),
            (s_pvn, s_pvn.count), (s_xte, s_xte.count), (s_ma, s_ma.count),
            (s_mb, s_mb.count), (s_gst[0], P0B_ST0), (s_gst[1], P0B_ST1),
            (s_gv, P0B_GV), (s_gev, P0B_GEV), (s_ga, s_ga.count)],
      inc=(s_w2, 16))
    E("scalar", lambda e: e.dma_start(
        w2u.ap().rearrange("p (c n) -> p c n", c=FC),
        wT["wuT"].rearrange("(c p) n -> p c n", p=128)),
      wait=[], inc=(s_w2, 16))
    E("scalar", lambda e: e.dma_start(
        w2d.ap().rearrange("p (c n) -> p c n", c=ICC),
        wT["wdT"].rearrange("(c p) n -> p c n", p=128)),
      wait=[], inc=(s_w2, 16))
    W2C = s_w2.count

    qp_hh = [0]
    xq_gp6 = [0]
    IC_SIZES = [512, 512, 512, 512, 512, 256]
    IC_OFF = [0, 512, 1024, 1536, 2048, 2560]
    NGT = SEQ_PER_CORE * TT

    def stageA(t):
        sqi, tt = divmod(t, TT)
        bufi = t % 2
        E("sync", lambda e, o=xld[bufi].ap(),
          i=x2s[sqi, tt * 128:(tt + 1) * 128, :]: e.dma_start(o, i),
          wait=X2W_W + [(s_ap2, max(0, s_ap2.count - 1)),
                        (s_ov, s_ov.count)],
          inc=(s_x2r[bufi], 16))
        xc = s_x2r[bufi].count
        E("vector", lambda e, bufi=bufi, t=t: ts_(
            e, xln2.ap(), xld[bufi].ap(),
            ln2m.ap()[:, t:t + 1], ln2r.ap()[:, t:t + 1],
            ALU.subtract, ALU.mult),
          wait=[(s_x2r[bufi], xc), (s_ln, P1_LN), (s_qv, xq_gp6[0]),
                (s_w2, W2C)],
          inc=(s_ap2, 1))
        apc = s_ap2.count
        kq, xq_gp6[0] = gen_quant_dve(xln2.ap(), H, xq_sb.ap(),
                                      pre_wait=[(s_ap2, apc)],
                                      out_wait=[(s_t2, s_t2.count)])
        q2_flush()
        qn = q2_st["out_cnt"][kq]
        for k in range(FC):
            E("tensor", lambda e, k=k: e.transpose(
                psT_bf[:, k * 128:(k + 1) * 128],
                xq_sb.ap()[:, k * 128:(k + 1) * 128], ident_sb.ap()),
              wait=([(s_qv, qn)] + b.ps_wait(PST)) if k == 0 else [],
              inc=((s_t2, 1) if k == FC - 1 else None))
        tc = s_t2.count
        E("scalar", lambda e: e.copy(xqT.ap(), psT_bf[:, 0:1024]),
          wait=[(s_t2, tc), (s_gu, s_gu.count)], inc=(s_te2, 1))
        b.ps_read(PST, s_te2, s_te2.count)
        return s_te2.count

    def stageB(t, xqtc):
        hm_ic = []
        for ic in range(6):
            icw, ico = IC_SIZES[ic], IC_OFF[ic]
            par = ic % 2
            psg, psgn = (psX1, "psX1") if par == 0 else (psn["psS10"],
                                                        "psS10")
            psu, psun = ((psn["psS00"], "psS00") if par == 0
                         else (psn["psS11"], "psS11"))
            for (psx, psxn, wsb) in ((psg, psgn, w2g), (psu, psun, w2u)):
                for fc in range(FC):
                    E("tensor", lambda e, psx=psx, wsb=wsb, fc=fc, ico=ico,
                      icw=icw: e.matmul(
                        psx[:, 0:icw],
                        xqT.ap().rearrange("p (c t) -> p c t", c=FC)
                        [:, fc, :],
                        wsb.ap().rearrange("p (c n) -> p c n", c=FC)
                        [:, fc, ico:ico + icw],
                        start=(fc == 0), stop=(fc == FC - 1),
                        skip_group_check=True),
                      wait=([(s_te2, xqtc)] + b.ps_wait(psxn))
                      if fc == 0 else [],
                      inc=((s_gu, 1) if fc == FC - 1 else None))
            guc = s_gu.count
            E("scalar", lambda e, icw=icw, psg=psg, par=par: e.activation(
                sg_sb[par].ap()[:, 0:icw], psg[:, 0:icw], AF.Silu),
              wait=[(s_gu, guc), (s_hm, max(0, s_hm.count - 1))],
              inc=(s_sil, 1))
            b.ps_read(psgn, s_sil, s_sil.count)
            slc = s_sil.count
            E("vector", lambda e, icw=icw, ico=ico, psu=psu, par=par:
              e.tensor_tensor(
                hh.ap()[:, ico:ico + icw], sg_sb[par].ap()[:, 0:icw],
                psu[:, 0:icw], ALU.mult),
              wait=[(s_sil, slc)] + ([(s_qv, qp_hh[0])] if ic == 0 else []),
              inc=(s_hm, 1))
            b.ps_read(psun, s_hm, s_hm.count)
            hm_ic.append(s_hm.count)
        return hm_ic

    def stageC(t, hm_ic):
        ks = []
        for ci, cc0 in enumerate(range(0, II, CWH)):
            cw = min(CWH, II - cc0)
            hmw = hm_ic[min(2 * ci + 1, 5)]
            kq, _ = gen_quant_dve(hh.ap()[:, cc0:cc0 + cw], cw,
                                  hq.ap()[:, cc0:cc0 + cw],
                                  pre_wait=[(s_hm, hmw)],
                                  out_wait=[(s_t2, s_t2.count)])
            ks.append(kq)
        q2_flush()
        qp_hh[0] = q2_st["v_cnt"][ks[-1]]
        return [q2_st["out_cnt"][kq] for kq in ks]

    def stageD(t, hqns):
        sqi, tt = divmod(t, TT)
        bufi = t % 2
        for grp in range(4):
            k0, k1 = grp * 6, min(grp * 6 + 6, ICC)
            for k in range(k0, k1):
                E("tensor", lambda e, k=k, k0=k0: e.transpose(
                    psT_bf[:, (k - k0) * 128:(k - k0 + 1) * 128],
                    hq.ap()[:, k * 128:(k + 1) * 128], ident_sb.ap()),
                  wait=([(s_qv, hqns[min(grp, 2)])] + b.ps_wait(PST))
                  if k == k0 else [],
                  inc=((s_t2, 1) if k == k1 - 1 else None))
            tc = s_t2.count
            E("scalar", lambda e, k0=k0, k1=k1: e.copy(
                hqT.ap()[:, 0:(k1 - k0) * 128],
                psT_bf[:, 0:(k1 - k0) * 128]),
              wait=[(s_t2, tc)], inc=(s_te2, 1))
            b.ps_read(PST, s_te2, s_te2.count)
            hqt = s_te2.count
            for k in range(k0, k1):
                for ncb in range(2):
                    psDk, psDn = ((psP[0], "psP0") if ncb == 0
                                  else (psP[1], "psP1"))
                    wdn = []
                    if k == k0 and ncb == 0:
                        wdn = [(s_te2, hqt)]
                    if k == 0:
                        wdn = wdn + b.ps_wait(psDn)
                    E("tensor", lambda e, k=k, k0=k0, ncb=ncb, psDk=psDk:
                      e.matmul(
                        psDk[:, :],
                        hqT.ap()[:, (k - k0) * 128:(k - k0 + 1) * 128],
                        w2d.ap().rearrange("p (c n) -> p c n", c=ICC)
                        [:, k, ncb * 512:(ncb + 1) * 512],
                        start=(k == 0), stop=(k == ICC - 1), skip_group_check=True),
                      wait=wdn,
                      inc=((s_dn, 1) if (k == ICC - 1 and ncb == 1) else None))
        dnc = s_dn.count
        ob = x2w[bufi]
        E("vector", lambda e, ob=ob, bufi=bufi: e.tensor_tensor(
            ob.ap()[:, 0:512], psP[0][:, :], xld[bufi].ap()[:, 0:512],
            ALU.add),
          wait=[(s_dn, dnc), (s_ow[bufi], s_ow[bufi].count)], inc=(s_ov, 1))
        E("vector", lambda e, ob=ob, bufi=bufi: e.tensor_tensor(
            ob.ap()[:, 512:1024], psP[1][:, :], xld[bufi].ap()[:, 512:1024],
            ALU.add),
          wait=[(s_ov, s_ov.count)], inc=(s_ov, 1))
        ovc = s_ov.count
        b.ps_read("psP0", s_ov, ovc)
        b.ps_read("psP1", s_ov, ovc)
        E("sync", lambda e, ob=ob, sqi=sqi, tt=tt: e.dma_start(
            out_hbm[sqi, tt * 128:(tt + 1) * 128, :], ob.ap()),
          wait=[(s_ov, ovc)], inc=(s_ow[bufi], 16))

    xqtc = stageA(0)
    for t in range(NGT):
        hmc = stageB(t, xqtc)
        if t + 1 < NGT:
            xqtc = stageA(t + 1)
        hqns = stageC(t, hmc)
        stageD(t, hqns)

    OW0, OW1 = s_ow[0].count, s_ow[1].count
    E("sync", lambda e: e.wait_ge(b.sems["ow0"].h, OW0))
    E("sync", lambda e: e.wait_ge(b.sems["ow1"].h, OW1))

    assert b.unit_cursor_early == len(b.units) == b.unit_cursor_late, \
        "undripped deferred units remain"
    for s in b.sems.values():
        s.h = nc.semaphore(s.name).__enter__()

    block = nc.Block().__enter__()

    def runner(lst):
        def _(eng):
            for fn, waits, inc in lst:
                for (s, v) in waits:
                    if v > 0:
                        eng.wait_ge(s.h, v)
                inst = fn(eng)
                if inc is not None:
                    inst.then_inc(inc[0].h, inc[1])
        return _

    block.sync(runner(b.ops["sync"]))
    block.scalar(runner(b.ops["scalar"]))
    block.vector(runner(b.ops["vector"]))
    block.tensor(runner(b.ops["tensor"]))
    block.gpsimd(runner(b.ops["gpsimd"]))
    return nc


_cache = {}


def _build_masks():
    if "m" not in _cache:
        i = np.arange(S)[:, None]
        j = np.arange(S)[None, :]
        causal = j <= i
        strided = causal & ((i - j < STRIDE) | (j % STRIDE == i % STRIDE))
        fixed = causal & ((j // STRIDE == i // STRIDE) |
                          (j % STRIDE >= STRIDE - 1))
        _cache["m"] = (
            np.ascontiguousarray(strided.T.astype(ml_dtypes.bfloat16)),
            np.ascontiguousarray(fixed.T.astype(ml_dtypes.bfloat16)))
    return _cache["m"]


def kernel(L0_attn_x, L0_attn_Wq, L0_attn_Wk, L0_attn_Wv, L0_attn_Wo,
           L0_ffn_w_gate, L0_ffn_w_up, L0_ffn_w_down):
    x = np.ascontiguousarray(np.asarray(L0_attn_x, dtype=np.float32))
    mS, mF = _build_masks()
    if "nc" not in _cache:
        _cache["nc"] = build_program()
    nc = _cache["nc"]
    common = {
        "wq": np.ascontiguousarray(np.asarray(L0_attn_Wq, np.float32)),
        "wk": np.ascontiguousarray(np.asarray(L0_attn_Wk, np.float32)),
        "wv": np.ascontiguousarray(np.asarray(L0_attn_Wv, np.float32)),
        "wo": np.ascontiguousarray(np.asarray(L0_attn_Wo, np.float32)),
        "wg": np.ascontiguousarray(np.asarray(L0_ffn_w_gate, np.float32)),
        "wu": np.ascontiguousarray(np.asarray(L0_ffn_w_up, np.float32)),
        "wd": np.ascontiguousarray(np.asarray(L0_ffn_w_down, np.float32)),
        "maskS": mS, "maskF": mF,
        "ident": np.eye(128, dtype=ml_dtypes.bfloat16),
    }
    in_maps = []
    for c in range(NCORES):
        m = dict(common)
        m["x"] = np.ascontiguousarray(x[c * SEQ_PER_CORE:(c + 1) * SEQ_PER_CORE])
        in_maps.append(m)
    res = run_bass_kernel_spmd(nc, in_maps, core_ids=list(range(NCORES)))
    out = np.concatenate([r["out"] for r in res.results], axis=0)
    return out.astype(np.float32)


# version 11 (final)
# speedup vs baseline: 1.0215x; 1.0215x over previous
import math

import numpy as np
import ml_dtypes

import concourse.bass as bass
import concourse.mybir as mybir
from concourse.bass_utils import run_bass_kernel_spmd

F32 = mybir.dt.float32
BF16 = mybir.dt.bfloat16
U8 = mybir.dt.uint8
ALU = mybir.AluOpType
AX = mybir.AxisListType
AF = mybir.ActivationFunctionType

B = 16
S = 2048
H = 1024
NH = 16
HD = 64
II = 2816
NCORES = 8
SEQ_PER_CORE = B // NCORES
TT = S // 128
FC = H // 128
ICC = II // 128
STRIDE = max(1, int(math.isqrt(S)))
EXPB = 12.0
SCALE = HD ** -0.5

NOLAG = False
RC0 = 1.5 * (2.0 ** 23) * 0.5
RC1 = 1.5 * (2.0 ** 23) * 1.0
RC2 = 1.5 * (2.0 ** 23) * 2.0

QCW = 512
NQC = S // QCW
CW = 1024


class Sem:
    __slots__ = ("name", "count", "h")

    def __init__(self, name):
        self.name = name
        self.count = 0
        self.h = None


def ts_(e, out, in0, s1, s2, op0, op1=None):
    if s2 is None:
        return e.tensor_single_scalar(out, in0, s1, op0)
    return e.tensor_scalar(out, in0, s1, s2, op0, op1)


COMPUTE_ENGS = ("scalar", "vector", "gpsimd")
PE_ENGS = ("tensor", "sync")


class Builder:
    def __init__(self, debug=False):
        self.nc = bass.Bass("TRN2", target_bir_lowering=False)
        self.debug = debug
        self.ops = {"sync": [], "scalar": [], "vector": [], "tensor": [],
                    "gpsimd": []}
        self.units = []
        self.cur_unit = None
        self.unit_cursor_early = 0
        self.unit_cursor_late = 0
        self.sems = {}
        self._vsb = []
        self._off = 0
        self._ovl_base = None
        self._ovl_max = 0
        self.guard = {}

    def sem(self, name):
        if name not in self.sems:
            self.sems[name] = Sem(name)
        return self.sems[name]

    def _alloc(self, name, shape, dtype, align=64):
        per = int(np.prod(shape[1:])) * mybir.dt.size(dtype)
        off = (self._off + align - 1) // align * align
        self._off = off + per
        h = self.nc.virtual_sbuf_tensor(name, list(shape), dtype)
        self._vsb.append((h, off))
        return h

    def alias_sbuf(self, name, shape, dtype, of_handle):
        off = next(o for h, o in self._vsb if h is of_handle)
        h = self.nc.virtual_sbuf_tensor(name, list(shape), dtype)
        self._vsb.append((h, off))
        return h

    def finalize_arena(self):
        self._ovl_max = max(self._ovl_max, self._off)
        size = (self._ovl_max + 63) // 64 * 64
        assert size <= 212800, f"arena too big: {size} B/partition"
        nc = self.nc
        self._backing_cm = nc.sbuf_tensor("arena", [128, size], U8)
        self._backing = self._backing_cm.__enter__()
        base = nc.lookup_mloc(self._backing).addr
        for h, off in self._vsb:
            nc._tensor(h.name, h.shape, h.dtype, type="SB")
            mloc = nc.lookup_mloc(h)
            mloc.allocated = True
            mloc.addr = base + off
        return size

    def E(self, eng, fn, wait=(), inc=None, defer=False, late=False):
        if inc is not None:
            inc[0].count += inc[1]
        rec = (fn, [w for w in wait if w is not None], inc)
        if defer:
            self.cur_unit["late" if late else "early"][eng].append(rec)
        else:
            self.ops[eng].append(rec)

    def open_unit(self):
        self.cur_unit = {"early": {k: [] for k in self.ops},
                         "late": {k: [] for k in self.ops}, "gev": 0}

    def close_unit(self, gev):
        self.cur_unit["gev"] = gev
        self.units.append(self.cur_unit)
        self.cur_unit = None

    def _emit_late(self, k, state):
        u = self.units[k]
        for eng, lst in u["late"].items():
            for fn, waits, inc in lst:
                if eng == "tensor" and state["first_pe"] \
                        and state["pe_wait"] is not None:
                    waits = list(waits) + [state["pe_wait"]]
                    state["first_pe"] = False
                self.ops[eng].append((fn, waits, inc))
        state["last_gev"] = u["gev"]
        self.unit_cursor_late = k + 1

    def drip_units(self, n, pe_wait=None, ps_guard_cb=None):
        e0, e1 = self.unit_cursor_early, min(self.unit_cursor_early + n,
                                             len(self.units))
        state = {"first_pe": True, "pe_wait": pe_wait, "last_gev": None}
        for k in range(e0, e1):
            for eng, lst in self.units[k]["early"].items():
                self.ops[eng].extend(lst)
            for lk in range(self.unit_cursor_late, k - 1):
                self._emit_late(lk, state)
        self.unit_cursor_early = e1
        if state["last_gev"] is not None and ps_guard_cb is not None:
            ps_guard_cb(state["last_gev"])

    def drip_flush(self, pe_wait=None, ps_guard_cb=None):
        state = {"first_pe": True, "pe_wait": pe_wait, "last_gev": None}
        for k in range(self.unit_cursor_early, len(self.units)):
            for eng, lst in self.units[k]["early"].items():
                self.ops[eng].extend(lst)
            for lk in range(self.unit_cursor_late, k - 1):
                self._emit_late(lk, state)
        self.unit_cursor_early = len(self.units)
        for lk in range(self.unit_cursor_late, len(self.units)):
            self._emit_late(lk, state)
        if state["last_gev"] is not None and ps_guard_cb is not None:
            ps_guard_cb(state["last_gev"])

    def dma(self, eng, out_ap, in_ap, semm, wait=()):
        self.E(eng, lambda e, o=out_ap, i=in_ap: e.dma_start(o, i), wait=wait,
               inc=(semm, 16))
        return semm.count

    def ps_wait(self, name):
        w = self.guard.get(name, [])
        self.guard[name] = []
        return w

    def ps_read(self, name, semm, count):
        self.guard.setdefault(name, []).append((semm, count))


def build_program(debug=False):
    b = Builder(debug)
    nc = b.nc
    E, sem = b.E, b.sem

    x_in = nc.dram_tensor("x", [SEQ_PER_CORE, S, H], F32, kind="ExternalInput").ap()
    w_in = {}
    for nm, shp in (("wq", [H, H]), ("wk", [H, H]), ("wv", [H, H]), ("wo", [H, H]),
                    ("wg", [II, H]), ("wu", [II, H]), ("wd", [H, II])):
        w_in[nm] = nc.dram_tensor(nm, shp, F32, kind="ExternalInput").ap()
    maskS = nc.dram_tensor("maskS", [S, S], BF16, kind="ExternalInput").ap()
    maskF = nc.dram_tensor("maskF", [S, S], BF16, kind="ExternalInput").ap()
    ident_in = nc.dram_tensor("ident", [128, 128], BF16, kind="ExternalInput").ap()
    out_hbm = nc.dram_tensor("out", [SEQ_PER_CORE, S, H], F32,
                             kind="ExternalOutput").ap()
    v_hbm = nc.dram_tensor("vh", [S, NH * 65], BF16, kind="Internal").ap()
    x2s = nc.dram_tensor("x2s", [SEQ_PER_CORE, S, H], F32,
                         kind="ExternalOutput" if debug else "Internal").ap()
    wT = {}
    for nm, shp in (("wqT", [H, H]), ("wkT", [H, H]), ("wvT", [H, H]),
                    ("woT", [H, H]), ("wgT", [H, II]), ("wuT", [H, II]),
                    ("wdT", [II, H])):
        wT[nm] = nc.dram_tensor(nm + "s", shp, BF16, kind="Internal").ap()
    dbg = {}
    if debug:
        for nm, shp, dt_ in (("d_bigT", [128, FC * S], BF16),
                             ("d_atok", [128, TT * H], BF16)):
            dbg[nm] = nc.dram_tensor(nm, shp, dt_, kind="ExternalOutput").ap()

    A = b._alloc
    ident_sb = A("ident_sb", [128, 128], BF16)
    consts = A("consts", [128, 8], F32)
    lnsm = A("lnsm", [128, 16], F32)
    ln2m = A("ln2m", [128, SEQ_PER_CORE * TT], F32)
    ln2r = A("ln2r", [128, SEQ_PER_CORE * TT], F32)
    qs_sm = [[A(f"qs_{nm}{i}", [128, 64], F32)
              for nm in ("bm", "bmp", "inv", "i6", "s2")] for i in range(2)]
    xld = [A(f"xld{i}", [128, H], F32) for i in range(2)]
    x2w = [A(f"x2w{i}", [128, H], F32) for i in range(2)]
    pvr = A("pvr", [128, 16], F32)
    zbf = A("zbf", [128, 272], BF16)

    b._ovl_base = b._off
    g_f = [A(f"g_f{i}", [128, CW], F32) for i in range(6)]
    g_out = [A(f"g_out{i}", [128, CW], BF16) for i in range(2)]
    g_sm = [[A(f"g_{nm}{i}", [128, CW // 16], F32)
             for nm in ("bm", "bmp", "inv", "i6", "s2")] for i in range(2)]
    g_st = A("g_st", [128, CW], F32)
    g_s2f = A("g_s2f", [128, CW], F32)
    g_stbf = b.alias_sbuf("g_stbf", [128, 2 * CW], BF16, g_st)
    g_ev = [A(f"g_ev{i}", [128, 1024], BF16) for i in range(2)]

    bigT = A("bigT", [128, FC * S], BF16)
    atok = A("atok", [128, TT * H], BF16)
    vaug = A("vaug", [128, TT * NH * 65], BF16)
    vsum = A("vsum", [128, NH * 65], BF16)
    qtp = A("qtp", [128, S], BF16)
    ktp = A("ktp", [128, S], BF16)
    wslq = [A(f"wslq{i}", [128, FC * 128], BF16) for i in range(2)]
    wslk = [A(f"wslk{i}", [128, FC * 128], BF16) for i in range(2)]
    wfull = A("wfull", [128, FC * H], BF16)
    ptr_ = [A(f"ptr{i}", [128, QCW], BF16) for i in range(2)]
    ptm = [A(f"ptm{i}", [128, QCW], BF16) for i in range(4)]
    mbuf = [A(f"mbuf{i}", [128, QCW], BF16) for i in range(2)]
    xlnb = [A(f"xlnb{i}", [128, H], BF16) for i in range(2)]
    dum_act = b.alias_sbuf("dum_act", [128, H], F32, ptr_[0])
    p1_end = b._off

    b._ovl_max = max(b._ovl_max, b._off)
    b._off = b._ovl_base
    w2g = A("w2g", [128, FC * II], BF16)
    w2u = A("w2u", [128, FC * II], BF16)
    w2d = A("w2d", [128, ICC * H], BF16)
    CWH = 1024
    p2_f = [A(f"p2_f{i}", [128, CWH], F32) for i in range(6)]
    hh = A("hh", [128, II], F32)
    hq = A("hq", [128, II], BF16)
    hqT = A("hqT", [128, 768], BF16)
    xq_sb = A("xq_sb", [128, H], BF16)
    xqT = A("xqT", [128, H], BF16)
    xln2 = A("xln2", [128, H], F32)
    sg_sb = [A(f"sg_sb{i}", [128, 512], F32) for i in range(2)]

    arena_size = b.finalize_arena()

    psn = {}
    for nm in ("psS00", "psS01", "psS10", "psS11", "psP0", "psP1",
               "psX0", "psX1"):
        psn[nm] = nc.psum_tensor(nm, [128, 512], F32).__enter__()
    psS = [[psn["psS00"], psn["psS01"]], [psn["psS10"], psn["psS11"]]]
    psSn = [["psS00", "psS01"], ["psS10", "psS11"]]
    psP = [psn["psP0"], psn["psP1"]]
    psX0, psX1 = psn["psX0"], psn["psX1"]
    tr_h = nc.virtual_psum_tensor("psTbf", [128, 1024], BF16)
    nc._tensor(tr_h.name, tr_h.shape, tr_h.dtype, type="PSUM")
    _m = nc.lookup_mloc(tr_h)
    _m.allocated = True
    _m.addr = nc.lookup_mloc(psX0).addr
    _m.bank = nc.lookup_mloc(psX0).bank
    psT_bf = tr_h.ap()
    PST = "psX0"
    tr2_h = nc.virtual_psum_tensor("psT2bf", [128, 1024], BF16)
    nc._tensor(tr2_h.name, tr2_h.shape, tr2_h.dtype, type="PSUM")
    _m2 = nc.lookup_mloc(tr2_h)
    _m2.allocated = True
    _m2.addr = nc.lookup_mloc(psn["psP0"]).addr
    _m2.bank = nc.lookup_mloc(psn["psP0"]).bank
    psT2_bf = tr2_h.ap()

    s_qv = sem("qv")
    s_qa = sem("qa")
    s_qp = sem("qp")
    s_gv = sem("gv")
    s_gp = sem("gp")
    s_ga = sem("ga")

    def make_quant(F, SM, s_v, s_p, defer):
        st = {"k": 0, "gp19": []}

        def gen(src_ap, n, out_ap, pre_wait=(), out_wait=()):
            k = st["k"]
            st["k"] += 1
            nb = n // 16
            f = [h.ap()[:, 0:n] for h in F]
            bm, bmp, inv, i6, s2 = (h.ap()[:, 0:nb] for h in SM[k % 2])

            def sv(fn, extra=()):
                E("vector", fn, wait=[(s_v, s_v.count)] + list(extra),
                  inc=(s_v, 1), defer=defer)

            def sp(fn, extra=()):
                E("gpsimd", fn, wait=[(s_p, s_p.count)] + list(extra),
                  inc=(s_p, 1), defer=defer)

            w_sm = [(s_p, st["gp19"][k - 2])] if k >= 2 else []
            sv(lambda e: e.tensor_reduce(
                bm, src_ap.rearrange("p (bb kk) -> p bb kk", kk=16),
                AX.X, ALU.max, apply_absolute_value=True),
               extra=list(pre_wait) + w_sm)
            sv(lambda e: ts_(e, bmp, bm, 6e-12, None, ALU.max))
            sv(lambda e: e.reciprocal(inv, bmp))
            sv(lambda e: ts_(e, i6, inv, 6.0, None, ALU.mult))
            sv(lambda e: ts_(e, s2, bmp, 1.0 / 6.0, None, ALU.mult))
            gv5 = s_v.count
            sp(lambda e, nb_=nb: e.tensor_tensor(
                f[0].rearrange("p (bb kk) -> p kk bb", bb=nb_),
                src_ap.rearrange("p (bb kk) -> p kk bb", bb=nb_),
                i6.rearrange("p (o bb) -> p o bb", o=1)
                .broadcast_to([128, 16, nb_]), ALU.mult),
               extra=[(s_v, gv5)])
            gp6 = s_p.count
            sv(lambda e: ts_(e, f[1], f[0], RC0, RC0, ALU.add, ALU.subtract),
               extra=[(s_p, gp6)])
            sv(lambda e: ts_(e, f[2], f[0], RC1, RC1, ALU.add, ALU.subtract))
            sv(lambda e: ts_(e, f[3], f[0], RC2, RC2, ALU.add, ALU.subtract))
            sv(lambda e: ts_(e, f[0], f[1], -2.0, 2.0, ALU.max, ALU.min))
            sv(lambda e: ts_(e, f[1], f[2], -4.0, 4.0, ALU.max, ALU.min))
            sv(lambda e: ts_(e, f[4], f[2], -2.0, 2.0, ALU.max, ALU.min))
            sv(lambda e: ts_(e, f[2], f[3], -6.0, 6.0, ALU.max, ALU.min))
            sv(lambda e: ts_(e, f[5], f[3], -4.0, 4.0, ALU.max, ALU.min))
            gv14 = s_v.count
            sp(lambda e: e.tensor_tensor(f[3], f[1], f[4], ALU.subtract),
               extra=[(s_v, gv14)])
            sp(lambda e: e.tensor_tensor(f[1], f[2], f[5], ALU.subtract))
            sp(lambda e: e.tensor_tensor(f[4], f[3], f[1], ALU.add))
            sp(lambda e: e.tensor_tensor(f[2], f[4], f[0], ALU.add))
            sp(lambda e, nb_=nb: e.tensor_tensor(
                out_ap.rearrange("p (bb kk) -> p kk bb", bb=nb_),
                f[2].rearrange("p (bb kk) -> p kk bb", bb=nb_),
                s2.rearrange("p (o bb) -> p o bb", o=1)
                .broadcast_to([128, 16, nb_]), ALU.mult),
               extra=list(out_wait))
            st["gp19"].append(s_p.count)
            return s_p.count, gp6
        return gen


    def make_quant_dve(F, SM, s_v, s_a, S2F=None, defer=False,
                       act_i6=True):
        st = {"k": 0, "v_cnt": [], "out_cnt": [], "t3": 0, "S": 0,
              "out": 0, "a_cnt": []}

        def sv(fn, extra=(), chain=True):
            E("vector", fn,
              wait=([(s_v, s_v.count)] if chain else []) + list(extra),
              inc=(s_v, 1), defer=defer)

        def sa(fn, extra=()):
            E("scalar", fn, wait=list(extra), inc=(s_a, 1), defer=defer)

        def gen(src_ap, n, out_ap, pre_wait=(), out_wait=()):
            k = st["k"]
            st["k"] += 1
            nb = n // 16
            f = [h.ap()[:, 0:n] for h in F]
            bm, bmp, inv, i6, s2 = (h.ap()[:, 0:nb] for h in SM[k % 2])
            w_sm = ([(s_v, st["out_cnt"][k - 2]),
                     (s_a, st["a_cnt"][k - 2])] if k >= 2 else [])
            sv(lambda e: e.tensor_reduce(
                bm, src_ap.rearrange("p (bb kk) -> p bb kk", kk=16),
                AX.X, ALU.max, apply_absolute_value=True),
               extra=list(pre_wait) + w_sm)
            sv(lambda e: ts_(e, bmp, bm, 6e-12, None, ALU.max))
            sv(lambda e: e.reciprocal(inv, bmp))
            sv(lambda e: ts_(e, i6, inv, 6.0, None, ALU.mult))
            sv(lambda e: ts_(e, s2, bmp, 1.0 / 6.0, None, ALU.mult))
            gv5 = s_v.count
            if act_i6:
                sa(lambda e, nb_=nb: e.activation(
                    f[0].rearrange("p (bb kk) -> p kk bb", bb=nb_),
                    i6.rearrange("p (o bb) -> p o bb", o=1)
                    .broadcast_to([128, 16, nb_]), AF.Copy),
                   extra=list(pre_wait) + [(s_v, max(gv5, st["t3"]))])
            ca1 = s_a.count
            if S2F is not None:
                s2f = S2F.ap()[:, 0:n]
                sa(lambda e, nb_=nb: e.activation(
                    s2f.rearrange("p (bb kk) -> p kk bb", bb=nb_),
                    s2.rearrange("p (o bb) -> p o bb", o=1)
                    .broadcast_to([128, 16, nb_]), AF.Copy),
                   extra=[(s_v, max(gv5, st["out"]))])
            ca2 = s_a.count
            if act_i6:
                sv(lambda e: e.tensor_tensor(f[1], src_ap, f[0], ALU.mult),
                   extra=[(s_a, ca1), (s_v, st["S"])], chain=False)
            else:
                sv(lambda e, nb_=nb: e.tensor_tensor(
                    f[1].rearrange("p (bb kk) -> p kk bb", bb=nb_),
                    src_ap.rearrange("p (bb kk) -> p kk bb", bb=nb_),
                    i6.rearrange("p (o bb) -> p o bb", o=1)
                    .broadcast_to([128, 16, nb_]), ALU.mult),
                   extra=[(s_v, max(gv5, st["t3"], st["S"]))], chain=False)
            cv = s_v.count
            sv(lambda e: ts_(e, f[2], f[1], RC0, RC0, ALU.add, ALU.subtract),
               extra=[(s_v, cv)], chain=False)
            cw0 = s_v.count
            sv(lambda e: ts_(e, f[3], f[1], RC1, RC1, ALU.add, ALU.subtract),
               extra=[(s_v, cv)], chain=False)
            cw1 = s_v.count
            sv(lambda e: ts_(e, f[4], f[1], RC2, RC2, ALU.add, ALU.subtract),
               extra=[(s_v, cv)], chain=False)
            cw2 = s_v.count
            sv(lambda e: ts_(e, f[5], f[2], -2.0, 2.0, ALU.max, ALU.min),
               extra=[(s_v, cw0)], chain=False)
            cc0 = s_v.count
            sv(lambda e: ts_(e, f[0], f[3], -4.0, 4.0, ALU.max, ALU.min),
               extra=[(s_v, max(cw1, cv))], chain=False)
            cA1 = s_v.count
            sv(lambda e: ts_(e, f[1], f[3], -2.0, 2.0, ALU.max, ALU.min),
               extra=[(s_v, cw2)], chain=False)
            cB1 = s_v.count
            sv(lambda e: ts_(e, f[2], f[4], -6.0, 6.0, ALU.max, ALU.min),
               extra=[(s_v, cc0)], chain=False)
            cA2 = s_v.count
            sv(lambda e: ts_(e, f[3], f[4], -4.0, 4.0, ALU.max, ALU.min),
               extra=[(s_v, cB1)], chain=False)
            cB2 = s_v.count
            sv(lambda e: e.tensor_tensor(f[4], f[0], f[1], ALU.subtract),
               extra=[(s_v, cB2)], chain=False)
            cu = s_v.count
            sv(lambda e: e.tensor_tensor(f[0], f[2], f[3], ALU.subtract),
               extra=[(s_v, cu)], chain=False)
            ct2 = st["t3"] = s_v.count
            sv(lambda e: e.tensor_tensor(f[1], f[4], f[0], ALU.add),
               extra=[(s_v, ct2)], chain=False)
            ct3 = s_v.count
            sv(lambda e: e.tensor_tensor(f[2], f[1], f[5], ALU.add),
               extra=[(s_v, ct3)], chain=False)
            cS = st["S"] = s_v.count
            if S2F is not None:
                sv(lambda e: e.tensor_tensor(out_ap, f[2], s2f, ALU.mult),
                   extra=[(s_v, cS), (s_a, ca2)] + list(out_wait),
                   chain=False)
            else:
                sv(lambda e, nb_=nb: e.tensor_tensor(
                    out_ap.rearrange("p (bb kk) -> p kk bb", bb=nb_),
                    f[2].rearrange("p (bb kk) -> p kk bb", bb=nb_),
                    s2.rearrange("p (o bb) -> p o bb", o=1)
                    .broadcast_to([128, 16, nb_]), ALU.mult),
                   extra=[(s_v, cS)] + list(out_wait), chain=False)
            st["out"] = s_v.count
            st["v_cnt"].append(cv)
            st["out_cnt"].append(s_v.count)
            st["a_cnt"].append(ca2)
            return k, cv
        return gen, st

    gen_quant_dve, q2_st = make_quant_dve(p2_f, qs_sm, s_qv, s_qa,
                                          act_i6=False)
    gen_quant_drip, gd_st = make_quant_dve(g_f, g_sm, s_gv, s_ga,
                                           S2F=g_s2f, defer=True)


    def q2_flush():
        pass

    s_wld = sem("wld")
    s_p0mm = sem("p0mm")
    s_p0ev = sem("p0ev")
    s_p0st = [sem("p0st0"), sem("p0st1")]
    s_ini = sem("ini")

    E("vector", lambda e: e.memset(consts.ap()[:, 0:1], 1e-5), inc=(s_ini, 1))
    E("vector", lambda e: e.memset(consts.ap()[:, 1:2], -EXPB),
      wait=[(s_ini, s_ini.count)], inc=(s_ini, 1))
    E("vector", lambda e: e.memset(zbf.ap(), 0.0),
      wait=[(s_ini, s_ini.count)], inc=(s_ini, 1))
    INI = s_ini.count
    s_idl = sem("idl")
    b.dma("sync", ident_sb.ap(), ident_in, s_idl)
    IDL = s_idl.count

    p0ev_attn = [A if False else None]
    for nm in ("wq", "wk", "wv", "wo"):
        src, dst = w_in[nm], wT[nm + "T"]
        for ot in range(FC):
            st_bf = g_stbf.ap()[:, 0:1024]
            E("gpsimd", lambda e, o=st_bf, i=src[ot * 128:(ot + 1) * 128, :]:
              e.dma_start(o, i), wait=[(s_p0mm, s_p0mm.count)], inc=(s_wld, 16))
            ld = s_wld.count
            for k in range(FC):
                E("tensor", lambda e, k=k, src_=st_bf: e.transpose(
                    psT_bf[:, k * 128:(k + 1) * 128],
                    src_[:, k * 128:(k + 1) * 128], ident_sb.ap()),
                  wait=([(s_wld, ld), (s_idl, IDL)] + b.ps_wait(PST))
                  if k == 0 else [],
                  inc=((s_p0mm, 1) if k == FC - 1 else None))
            mm = s_p0mm.count
            ev = g_ev[ot % 2]
            E("vector", lambda e, ev=ev: e.tensor_copy(
                ev.ap()[:, 0:1024], psT_bf[:, 0:1024]),
              wait=[(s_p0mm, mm), (s_p0st[ot % 2], s_p0st[ot % 2].count)],
              inc=(s_p0ev, 1))
            b.ps_read(PST, s_p0ev, s_p0ev.count)
            evc = s_p0ev.count
            dst_ap = bass.AP(dst.tensor, ot * 128,
                             [[H, 128], [128 * H, FC], [1, 128]])
            E("sync", lambda e, o=dst_ap, i=ev.ap()[:, 0:1024].rearrange(
                "p (k c) -> p k c", k=FC): e.dma_start(o, i),
              wait=[(s_p0ev, evc)], inc=(s_p0st[ot % 2], 16))
    P0A_ST0 = s_p0st[0].count
    P0A_ST1 = s_p0st[1].count
    P0A_W = [(s_p0st[0], P0A_ST0), (s_p0st[1], P0A_ST1)]

    s_gld = sem("gld")
    s_gmm = sem("gmm")
    s_gev = sem("gev")
    s_gst = [sem("gst0"), sem("gst1")]

    unit_no = [0]
    gp6_prev = [0]
    for nm, rows, cols in (("wg", II, H), ("wu", II, H), ("wd", H, II)):
        src, dst = w_in[nm], wT[nm + "T"]
        for ot in range(rows // 128):
            for cc0 in range(0, cols, CW):
                b.open_unit()
                unit_no[0] += 1
                cw = min(CW, cols - cc0)
                E("scalar", lambda e, o=g_st.ap()[:, 0:cw],
                  i=src[ot * 128:(ot + 1) * 128, cc0:cc0 + cw]:
                  e.dma_start(o, i),
                  wait=[(s_gv, gp6_prev[0])], inc=(s_gld, 16), defer=True)
                ld = s_gld.count
                qdst = g_out[unit_no[0] % 2].ap()[:, 0:cw]
                kq, gp6_prev[0] = gen_quant_drip(
                    g_st.ap()[:, 0:cw], cw, qdst,
                    pre_wait=[(s_gld, ld)],
                    out_wait=[(s_gmm, max(0, s_gmm.count - 1))])
                gqn = gd_st["out_cnt"][kq]
                nck = cw // 128
                for k in range(nck):
                    E("tensor", lambda e, k=k, qdst=qdst: e.transpose(
                        psT2_bf[:, k * 128:(k + 1) * 128],
                        qdst[:, k * 128:(k + 1) * 128], ident_sb.ap()),
                      wait=[(s_gv, gqn), (s_gev, s_gev.count)]
                      if k == 0 else [],
                      inc=((s_gmm, 1) if k == nck - 1 else None),
                      defer=True, late=True)
                mm = s_gmm.count
                ev = g_ev[unit_no[0] % 2]
                E("vector", lambda e, nck=nck, ev=ev: e.tensor_copy(
                    ev.ap()[:, 0:nck * 128], psT2_bf[:, 0:nck * 128]),
                  wait=[(s_gmm, mm),
                        (s_gst[unit_no[0] % 2], s_gst[unit_no[0] % 2].count)],
                  inc=(s_gev, 1), defer=True, late=True)
                evc = s_gev.count
                dst_ap = bass.AP(dst.tensor, cc0 * rows + ot * 128,
                                 [[rows, 128], [128 * rows, nck], [1, 128]])
                E("sync", lambda e, o=dst_ap,
                  i=ev.ap()[:, 0:nck * 128].rearrange("p (k c) -> p k c",
                                                      k=nck):
                  e.dma_start(o, i),
                  wait=[(s_gev, evc)], inc=(s_gst[unit_no[0] % 2], 16),
                  defer=True, late=True)
                b.close_unit(evc)

    import os as _os
    if _os.environ.get("K_NODRIP"):
        b.drip_units(len(b.units))
        b.drip_units(len(b.units))
    P0B_ST0 = s_gst[0].count
    P0B_ST1 = s_gst[1].count
    P0B_GP = s_gp.count
    P0B_GV = s_gv.count
    P0B_GEV = s_gev.count
    P0B_GMM = s_gmm.count

    s_x = [sem("x0"), sem("x1")]
    s_ln = sem("ln")
    s_lnv = sem("lnv")
    s_ap1 = sem("ap1")
    s_xt = sem("xt")
    s_xte = sem("xte")
    s_wf = sem("wf")
    s_vps = sem("vps")
    s_vev = sem("vev")
    s_qkl = [sem("qkl0"), sem("qkl1")]
    s_qkm = sem("qkm")
    s_qke = sem("qke")
    s_sa = sem("sa")
    s_sb = sem("sb")
    s_ea = sem("ea")
    s_eb = sem("eb")
    s_ma = sem("ma")
    s_mb = sem("mb")
    s_pva = sem("pva")
    s_pvb = sem("pvb")
    s_pvn = sem("pvn")
    s_mld = [sem("mld0"), sem("mld1")]
    s_wo = sem("wo")
    s_res = sem("res")
    s_x2w = [sem("x2w0"), sem("x2w1")]

    def ln_stats(xap, m_slot, r_slot, pre_wait):
        su = lnsm.ap()[:, 0:1]
        sq = lnsm.ap()[:, 1:2]
        var = lnsm.ap()[:, 3:4]
        lnv = lnsm.ap()[:, 4:5]
        E("scalar", lambda e: e.activation(dum_act.ap(), xap, AF.Copy,
                                           accum_out=su),
          wait=list(pre_wait) + [(s_lnv, s_lnv.count)], inc=(s_ln, 1))
        E("scalar", lambda e: e.activation(dum_act.ap(), xap, AF.Square,
                                           accum_out=sq),
          wait=[(s_ln, s_ln.count)], inc=(s_ln, 1))
        lc = s_ln.count
        E("vector", lambda e, m=m_slot: ts_(e, m, su, 1.0 / H, None, ALU.mult),
          wait=[(s_ln, lc)], inc=(s_lnv, 1))
        E("vector", lambda e, m=m_slot: e.tensor_tensor(var, m, m, ALU.mult),
          wait=[(s_lnv, s_lnv.count)], inc=(s_lnv, 1))
        E("vector", lambda e: e.scalar_tensor_tensor(var, sq, 1.0 / H, var,
                                                     ALU.mult, ALU.subtract),
          wait=[(s_lnv, s_lnv.count)], inc=(s_lnv, 1))
        vc = s_lnv.count
        E("scalar", lambda e: e.activation(lnv, var, AF.Ln,
                                           bias=consts.ap()[:, 0:1]),
          wait=[(s_lnv, vc), (s_ini, INI)], inc=(s_ln, 1))
        E("scalar", lambda e, r=r_slot: e.activation(r, lnv, AF.Exp,
                                                     scale=-0.5),
          wait=[(s_ln, s_ln.count)], inc=(s_ln, 1))
        return s_ln.count

    def emit_pv(hb, hg, tkind, tv, first, qlo, ktmax, pt_m, wsem, wcnt,
                pv0):
        s_pv = s_pva if hb == 0 else s_pvb
        psPn = "psP%d" % hb
        kt = tv if tkind == "band" else 0
        first_q = max(kt, qlo // 128) if tkind == "band" else qlo // 128
        for qg in range(first_q, ktmax):
            ql = qg - qlo // 128
            slot = ql * 65
            wpv = []
            if qg == first_q:
                wpv = [(wsem, wcnt), (s_vev, VEV[0])]
                if tkind == "sum":
                    wpv.append((s_vsm, VSM[0]))
                if first:
                    wpv += b.ps_wait(psPn)
            if tkind == "band":
                rhs = vaug.ap()[:, kt * NH * 65 + hg * 65:
                                kt * NH * 65 + hg * 65 + 65]
                lhs_np = 128
            else:
                rhs = vsum.ap()[0:tv, hg * 65:hg * 65 + 65]
                lhs_np = tv
            E("tensor", lambda e, pm=pt_m, qg=qg, qlo=qlo, rhs=rhs,
              hb=hb, slot=slot, ql=ql, lhs_np=lhs_np, kt=kt, first=first,
              tkind=tkind: e.matmul(
                psP[hb][:, slot:slot + 65],
                pm.ap()[0:lhs_np, qg * 128 - qlo:qg * 128 - qlo + 128],
                rhs,
                start=(first and ql == 0),
                stop=(tkind == "band" and kt == qg), skip_group_check=True),
              wait=wpv,
              inc=((s_pv, 1) if qg == ktmax - 1 else None))

    NPAIR = NH // 2
    mask_hist = []
    VEV = [0]
    VSM = [0]
    s_vhb = sem("vhb")
    s_vsm = sem("vsm")

    for sqi in range(SEQ_PER_CORE):
        for tt in range(TT):
            bufi = tt % 2
            wpre = [(s_ap1, max(0, s_ap1.count - 1)), (s_res, s_res.count)]
            if sqi == 0 and tt == 0:
                wpre.extend(P0A_W)
            E("sync", lambda e, o=xld[bufi].ap(),
              i=x_in[sqi, tt * 128:(tt + 1) * 128, :]: e.dma_start(o, i),
              wait=wpre, inc=(s_x[bufi], 16))
            xc = s_x[bufi].count
            xap = xld[bufi].ap()
            lc = ln_stats(xap, lnsm.ap()[:, 2:3], lnsm.ap()[:, 7:8],
                          [(s_x[bufi], xc)])
            xb_ = xlnb[bufi].ap()
            E("vector", lambda e, xa=xap, o=xb_: ts_(
                e, o, xa, lnsm.ap()[:, 2:3], lnsm.ap()[:, 7:8],
                ALU.subtract, ALU.mult),
              wait=[(s_ln, lc), (s_xt, max(0, s_xt.count - 1))], inc=(s_ap1, 1))
            ap_c = s_ap1.count
            for k in range(FC):
                E("tensor", lambda e, k=k, i=xb_[:, k * 128:(k + 1) * 128]:
                  e.transpose(psT_bf[:, k * 128:(k + 1) * 128], i,
                              ident_sb.ap()),
                  wait=([(s_ap1, ap_c)] + b.ps_wait(PST)) if k == 0 else [],
                  inc=((s_xt, 1) if k == FC - 1 else None))
            mmc = s_xt.count
            E("vector", lambda e, tt=tt: e.tensor_copy(
                bigT.ap().rearrange("p (c t) -> p c t", c=FC)
                [:, :, tt * 128:(tt + 1) * 128],
                psT_bf[:, 0:1024].rearrange("p (c t) -> p c t", c=FC)),
              wait=[(s_xt, mmc)], inc=(s_xte, 1))
            b.ps_read(PST, s_xte, s_xte.count)
        XTE = s_xte.count
        b.drip_units(2, pe_wait=(s_pvn, s_pvn.count),
                     ps_guard_cb=lambda g: b.ps_read("psP0", s_gev, g))

        E("scalar", lambda e: e.dma_start(
            wfull.ap().rearrange("p (c n) -> p c n", c=FC),
            wT["wvT"].rearrange("(c p) n -> p c n", p=128)),
          wait=P0A_W + [(s_qkm, s_qkm.count), (s_wo, s_wo.count)],
          inc=(s_wf, 16))
        wfc = s_wf.count
        E("vector", lambda e: e.memset(
            vaug.ap().rearrange("p (t h j) -> p t h j", h=NH, j=65)
            [:, :, :, 64:65], 1.0),
          wait=[(s_pvn, s_pvn.count)], inc=(s_vev, 1))
        for tt in range(TT):
            for nch in range(2):
                psV, psVn = (psX1, "psX1") if nch == 0 else (psn["psS00"],
                                                             "psS00")
                for fc in range(FC):
                    E("tensor", lambda e, tt=tt, fc=fc, nch=nch, psV=psV:
                      e.matmul(
                        psV[:, :],
                        bigT.ap().rearrange("p (c t) -> p c t", c=FC)
                        [:, fc, tt * 128:(tt + 1) * 128],
                        wfull.ap().rearrange("p (c n) -> p c n", c=FC)
                        [:, fc, nch * 512:(nch + 1) * 512],
                        start=(fc == 0), stop=(fc == FC - 1), skip_group_check=True),
                      wait=([(s_wf, wfc), (s_xte, XTE)] + b.ps_wait(psVn))
                      if fc == 0 else [],
                      inc=((s_vps, 1) if fc == FC - 1 else None))
                vp = s_vps.count
                E("vector", lambda e, tt=tt, nch=nch, psV=psV: e.tensor_copy(
                    vaug.ap().rearrange("p (t h j) -> p t h j", h=NH, j=65)
                    [:, tt, nch * 8:(nch + 1) * 8, 0:64],
                    psV[:, :].rearrange("p (h j) -> p h j", h=8)),
                  wait=[(s_vps, vp)], inc=(s_vev, 1))
                b.ps_read(psVn, s_vev, s_vev.count)
            vevc = s_vev.count
            E("sync", lambda e, tt=tt: e.dma_start(
                v_hbm[tt * 128:(tt + 1) * 128, :],
                vaug.ap()[:, tt * NH * 65:(tt + 1) * NH * 65]),
              wait=[(s_vev, vevc)] + ([(s_vsm, VSM[0])] if tt == 0 else []),
              inc=(s_vhb, 16))
        VEV[0] = s_vev.count
        E("sync", lambda e: e.dma_start(
            vsum.ap()[0:45, :],
            bass.AP(v_hbm.tensor, 44 * NH * 65,
                    [[45 * NH * 65, 45], [1, NH * 65]])),
          wait=[(s_vhb, s_vhb.count)], inc=(s_vsm, 16))
        VSM[0] = s_vsm.count

        for pair in range(NPAIR):
            mask_t = maskS if pair < NPAIR // 2 else maskF
            pc = pair * 128
            wbq, wbk = wslq[pair % 2], wslk[pair % 2]
            wq_ap = bass.AP(wT["wqT"].tensor, pc, [[H, 128], [128 * H, FC],
                                                   [1, 128]])
            wk_ap = bass.AP(wT["wkT"].tensor, pc, [[H, 128], [128 * H, FC],
                                                   [1, 128]])
            wpre = ([(s_qkm, max(0, s_qkm.count - 16))]
                    if (sqi * NPAIR + pair) >= 2 else list(P0A_W))
            s_qkl_p = s_qkl[pair % 2]
            E("scalar", lambda e, o=wbq.ap().rearrange("p (c n) -> p c n",
                                                       c=FC),
              i=wq_ap: e.dma_start(o, i), wait=wpre, inc=(s_qkl_p, 16))
            E("scalar", lambda e, o=wbk.ap().rearrange("p (c n) -> p c n",
                                                       c=FC),
              i=wk_ap: e.dma_start(o, i), wait=[], inc=(s_qkl_p, 16))
            qkc = s_qkl_p.count
            for (wb, outb) in ((wbq, qtp), (wbk, ktp)):
                for nchk in range(4):
                    psQ, psQn = ((psX1, "psX1") if nchk % 2 == 0
                                 else (psn["psS00"], "psS00"))
                    for fc in range(FC):
                        E("tensor", lambda e, wb=wb, nchk=nchk, fc=fc, psQ=psQ:
                          e.matmul(
                            psQ[:, :],
                            wb.ap().rearrange("p (c n) -> p c n", c=FC)
                            [:, fc, :],
                            bigT.ap().rearrange("p (c t) -> p c t", c=FC)
                            [:, fc, nchk * 512:(nchk + 1) * 512],
                            start=(fc == 0), stop=(fc == FC - 1), skip_group_check=True),
                          wait=([(s_qkl_p, qkc), (s_xte, XTE)] + b.ps_wait(psQn))
                          if fc == 0 else [],
                          inc=((s_qkm, 1) if fc == FC - 1 else None))
                    qm = s_qkm.count
                    E("vector", lambda e, outb=outb, nchk=nchk, psQ=psQ:
                      e.tensor_copy(outb.ap()[:, nchk * 512:(nchk + 1) * 512],
                                    psQ[:, :]),
                      wait=[(s_qkm, qm)], inc=(s_qke, 1))
                    b.ps_read(psQn, s_qke, s_qke.count)
            QKE = s_qke.count

            is_fixed = pair >= NPAIR // 2
            for qc in range(NQC):
                qlo, qhi = qc * QCW, qc * QCW + QCW
                ktmax = qhi // 128
                ktlo = max(0, (qlo - 44) // 128) if is_fixed else 0
                Mc = 0
                if is_fixed and ktlo > 0:
                    Mc = (ktlo * 128 - 45) // 45 + 1
                tiles = ([("sum", Mc)] if Mc > 0 else []) \
                    + [("band", kt) for kt in range(ktlo, ktmax)]
                pva0, pvb0 = s_pva.count, s_pvb.count
                prev = [None, None]
                for idx, (tkind, tv) in enumerate(tiles):
                    if tkind == "band":
                        kt = tv
                        qstart = max(qlo, kt * 128)
                        n = qhi - qstart
                        c0 = qstart - qlo
                        g = len(mask_hist)
                        mb_ = mbuf[g % 2]
                        wm = []
                        if g >= 2:
                            wm = [(s_ma, mask_hist[g - 2][0]),
                                  (s_mb, mask_hist[g - 2][1])]
                        E("sync", lambda e, o=mb_.ap()[:, 0:n],
                          i=mask_t[kt * 128:(kt + 1) * 128, qstart:qhi]:
                          e.dma_start(o, i), wait=wm, inc=(s_mld[g % 2], 16))
                        mldc = s_mld[g % 2].count
                        s_mld_g = s_mld[g % 2]
                        mask_hist.append(None)
                    for hb in (0, 1):
                        hg = pair * 2 + hb
                        psSh = psS[hb][idx % 2]
                        psShn = psSn[hb][idx % 2]
                        s_s = s_sa if hb == 0 else s_sb
                        s_e = s_ea if hb == 0 else s_eb
                        s_m = s_ma if hb == 0 else s_mb
                        s_pv = s_pva if hb == 0 else s_pvb
                        pv0 = pva0 if hb == 0 else pvb0
                        pt_raw = ptr_[hb]
                        pt_m = ptm[2 * hb + (idx % 2)]
                        if tkind == "band":
                            E("tensor", lambda e, kt=kt, hb=hb, c0=c0, n=n,
                              qs=qstart, psSh=psSh: e.matmul(
                                psSh[:, c0:c0 + n],
                                ktp.ap()[64 * hb:64 * hb + 64,
                                         kt * 128:(kt + 1) * 128],
                                qtp.ap()[64 * hb:64 * hb + 64, qs:qs + n],
                                start=True, stop=True, skip_group_check=True),
                              wait=[(s_qke, QKE)] + b.ps_wait(psShn),
                              inc=(s_s, 1))
                        else:
                            sum_ap = bass.AP(ktp.ap().tensor,
                                             (64 * hb) * S + 44,
                                             [[S, 64], [45, Mc]])
                            E("tensor", lambda e, hb=hb, psSh=psSh,
                              sum_ap=sum_ap, qlo=qlo, qhi=qhi, Mc=Mc:
                              e.matmul(
                                psSh[0:Mc, 0:QCW],
                                sum_ap,
                                qtp.ap()[64 * hb:64 * hb + 64, qlo:qhi],
                                start=True, stop=True, skip_group_check=True),
                              wait=[(s_qke, QKE)] + b.ps_wait(psShn),
                              inc=(s_s, 1))
                        sc = s_s.count
                        if prev[hb] is not None:
                            emit_pv(*prev[hb])
                        if tkind == "band":
                            E("scalar", lambda e, psSh=psSh, c0=c0, n=n,
                              pt=pt_raw: e.activation(
                                pt.ap()[:, c0:c0 + n], psSh[:, c0:c0 + n],
                                AF.Exp,
                                bias=consts.ap()[:, 1:2], scale=SCALE),
                              wait=[(s_s, sc), (s_m, s_m.count),
                                    (s_ini, INI)],
                              inc=(s_e, 1))
                            b.ps_read(psShn, s_e, s_e.count)
                            ec = s_e.count
                            wmm = [(s_e, ec), (s_mld_g, mldc)]
                            if idx >= 2:
                                wmm.append((s_pv, pv0 + idx - 1))
                            E("vector", lambda e, pt=pt_raw, pm=pt_m,
                              mb2=mb_, c0=c0, n=n: e.tensor_tensor(
                                pm.ap()[:, c0:c0 + n], pt.ap()[:, c0:c0 + n],
                                mb2.ap()[:, 0:n], ALU.mult),
                              wait=wmm, inc=(s_m, 1))
                            mc = s_m.count
                            prev[hb] = (hb, hg, "band", kt, idx == 0, qlo,
                                        ktmax, pt_m, s_m, mc, pv0)
                            mask_hist[g] = (s_ma.count, s_mb.count)
                        else:
                            drain = sc + 1
                            E("scalar", lambda e, psSh=psSh, pm=pt_m,
                              Mc=Mc: e.activation(
                                pm.ap()[0:Mc, 0:QCW], psSh[0:Mc, 0:QCW],
                                AF.Exp,
                                bias=consts.ap()[0:Mc, 1:2], scale=SCALE),
                              wait=[(s_s, drain), (s_m, s_m.count),
                                    (s_ini, INI)],
                              inc=(s_e, 1))
                            b.ps_read(psShn, s_e, s_e.count)
                            ec = s_e.count
                            prev[hb] = (hb, hg, "sum", Mc, idx == 0, qlo,
                                        ktmax, pt_m, s_e, ec, pv0)
                for hb in (0, 1):
                    if prev[hb] is not None:
                        emit_pv(*prev[hb])
                        prev[hb] = None
                for hb, s_pv in ((0, s_pva), (1, s_pvb)):
                    E("tensor", lambda e, hb=hb: e.matmul(
                        psP[hb][:, 0:260], zbf.ap()[:, 0:128],
                        zbf.ap()[:, 0:260], start=False, stop=True, skip_group_check=True),
                      wait=[(s_ini, INI)], inc=(s_pv, 1))
                for hb in (0, 1):
                    hg = pair * 2 + hb
                    s_pv = s_pva if hb == 0 else s_pvb
                    pvc = s_pv.count
                    for ql in range(QCW // 128):
                        qg = qlo // 128 + ql
                        slot = ql * 65
                        rr = pvr.ap()[:, hb * 8 + ql:hb * 8 + ql + 1]
                        w1_ = [(s_pvn, s_pvn.count)]
                        if ql == 0:
                            w1_.append((s_pv, pvc))
                        E("vector", lambda e, hb=hb, slot=slot, rr=rr:
                          e.reciprocal(rr, psP[hb][:, slot + 64:slot + 65]),
                          wait=w1_, inc=(s_pvn, 1))
                        E("vector", lambda e, hb=hb, slot=slot, rr=rr, qg=qg,
                          hg=hg: ts_(
                            e, atok.ap().rearrange("p (t c) -> p t c", t=TT)
                            [:, qg, hg * 64:(hg + 1) * 64],
                            psP[hb][:, slot:slot + 64], rr, None, ALU.mult),
                          wait=[(s_pvn, s_pvn.count)], inc=(s_pvn, 1))
                        b.ps_read("psP%d" % hb, s_pvn, s_pvn.count)
            b.drip_units(4, pe_wait=(s_pvn, s_pvn.count),
                         ps_guard_cb=lambda g: b.ps_read("psP0", s_gev, g))
        ATOK = s_pvn.count

        for tt in range(TT):
            for k in range(FC):
                E("tensor", lambda e, tt=tt, k=k: e.transpose(
                    psT_bf[:, k * 128:(k + 1) * 128],
                    atok.ap()[:, tt * 1024 + k * 128:tt * 1024 + (k + 1) * 128],
                    ident_sb.ap()),
                  wait=([(s_pvn, ATOK)] + b.ps_wait(PST)) if k == 0 else [],
                  inc=((s_xt, 1) if k == FC - 1 else None))
            atc = s_xt.count
            E("vector", lambda e, tt=tt: e.tensor_copy(
                bigT.ap().rearrange("p (c t) -> p c t", c=FC)
                [:, :, tt * 128:(tt + 1) * 128],
                psT_bf[:, 0:1024].rearrange("p (c t) -> p c t", c=FC)),
              wait=[(s_xt, atc)], inc=(s_xte, 1))
            b.ps_read(PST, s_xte, s_xte.count)
        ATE = s_xte.count

        E("scalar", lambda e: e.dma_start(
            wfull.ap().rearrange("p (c n) -> p c n", c=FC),
            wT["woT"].rearrange("(c p) n -> p c n", p=128)),
          wait=[(s_vps, s_vps.count), (s_qkm, s_qkm.count)], inc=(s_wf, 16))
        wfc = s_wf.count
        for tt in range(TT):
            bufi = tt % 2
            E("sync", lambda e, o=xld[bufi].ap(),
              i=x_in[sqi, tt * 128:(tt + 1) * 128, :]: e.dma_start(o, i),
              wait=[(s_ap1, s_ap1.count), (s_res, max(0, s_res.count - 2))],
              inc=(s_x[bufi], 16))
            xc = s_x[bufi].count
            for ncb in range(2):
                psW, psWn = ((psX1, "psX1") if ncb == 0
                             else (psn["psS00"], "psS00"))
                for fc in range(FC):
                    E("tensor", lambda e, tt=tt, fc=fc, ncb=ncb, psW=psW:
                      e.matmul(
                        psW[:, :],
                        bigT.ap().rearrange("p (c t) -> p c t", c=FC)
                        [:, fc, tt * 128:(tt + 1) * 128],
                        wfull.ap().rearrange("p (c n) -> p c n", c=FC)
                        [:, fc, ncb * 512:(ncb + 1) * 512],
                        start=(fc == 0), stop=(fc == FC - 1), skip_group_check=True),
                      wait=([(s_wf, wfc), (s_xte, ATE)] + b.ps_wait(psWn))
                      if fc == 0 else [],
                      inc=((s_wo, 1) if fc == FC - 1 else None))
            woc = s_wo.count
            x2b = x2w[bufi]
            E("vector", lambda e, x2b=x2b, bufi=bufi: e.tensor_tensor(
                x2b.ap()[:, 0:512], psX1[:, :], xld[bufi].ap()[:, 0:512],
                ALU.add),
              wait=[(s_wo, woc), (s_x[bufi], xc),
                    (s_x2w[bufi], s_x2w[bufi].count)],
              inc=(s_res, 1))
            E("vector", lambda e, x2b=x2b, bufi=bufi: e.tensor_tensor(
                x2b.ap()[:, 512:1024], psn["psS00"][:, :],
                xld[bufi].ap()[:, 512:1024], ALU.add),
              wait=[(s_res, s_res.count)], inc=(s_res, 1))
            rc = s_res.count
            b.ps_read("psX1", s_res, rc)
            b.ps_read("psS00", s_res, rc)
            gt = sqi * TT + tt
            lc = ln_stats(x2b.ap(), ln2m.ap()[:, gt:gt + 1],
                          ln2r.ap()[:, gt:gt + 1], [(s_res, rc)])
            E("sync", lambda e, x2b=x2b, sqi=sqi, tt=tt: e.dma_start(
                x2s[sqi, tt * 128:(tt + 1) * 128, :], x2b.ap()),
              wait=[(s_ln, lc)], inc=(s_x2w[bufi], 16))
        b.drip_units(3, pe_wait=(s_pvn, s_pvn.count),
                     ps_guard_cb=lambda g: b.ps_read("psP0", s_gev, g))
        if debug and sqi == 0:
            b.dma("sync", dbg["d_bigT"], bigT.ap(), sem("dbgs"),
                  wait=[(s_xte, ATE)])
            b.dma("sync", dbg["d_atok"], atok.ap(), sem("dbgs"),
                  wait=[(s_pvn, ATOK)])

    P1_X2W0 = s_x2w[0].count
    P1_X2W1 = s_x2w[1].count
    X2W_W = [(s_x2w[0], P1_X2W0), (s_x2w[1], P1_X2W1)]
    P1_LN = s_ln.count
    b.drip_flush(pe_wait=(s_pvn, s_pvn.count),
                 ps_guard_cb=lambda g: b.ps_read("psP0", s_gev, g))

    s_w2 = sem("w2")
    s_x2r = [sem("x2r0"), sem("x2r1")]
    s_t2 = sem("t2")
    s_te2 = sem("te2")
    s_gu = sem("gu")
    s_sil = sem("sil")
    s_hm = sem("hm")
    s_dn = sem("dn")
    s_ov = sem("ov")
    s_ow = [sem("ow0"), sem("ow1")]
    s_ap2 = sem("ap2")

    E("scalar", lambda e: e.dma_start(
        w2g.ap().rearrange("p (c n) -> p c n", c=FC),
        wT["wgT"].rearrange("(c p) n -> p c n", p=128)),
      wait=X2W_W + [(s_qkm, s_qkm.count), (s_wo, s_wo.count),
            (s_pvn, s_pvn.count), (s_xte, s_xte.count), (s_ma, s_ma.count),
            (s_mb, s_mb.count), (s_gst[0], P0B_ST0), (s_gst[1], P0B_ST1),
            (s_gv, P0B_GV), (s_gev, P0B_GEV), (s_ga, s_ga.count)],
      inc=(s_w2, 16))
    E("scalar", lambda e: e.dma_start(
        w2u.ap().rearrange("p (c n) -> p c n", c=FC),
        wT["wuT"].rearrange("(c p) n -> p c n", p=128)),
      wait=[], inc=(s_w2, 16))
    E("scalar", lambda e: e.dma_start(
        w2d.ap().rearrange("p (c n) -> p c n", c=ICC),
        wT["wdT"].rearrange("(c p) n -> p c n", p=128)),
      wait=[], inc=(s_w2, 16))
    W2C = s_w2.count

    qp_hh = [0]
    xq_gp6 = [0]
    IC_SIZES = [512, 512, 512, 512, 512, 256]
    IC_OFF = [0, 512, 1024, 1536, 2048, 2560]
    NGT = SEQ_PER_CORE * TT

    def stageA(t):
        sqi, tt = divmod(t, TT)
        bufi = t % 2
        E("sync", lambda e, o=xld[bufi].ap(),
          i=x2s[sqi, tt * 128:(tt + 1) * 128, :]: e.dma_start(o, i),
          wait=X2W_W + [(s_ap2, max(0, s_ap2.count - 1)),
                        (s_ov, s_ov.count)],
          inc=(s_x2r[bufi], 16))
        xc = s_x2r[bufi].count
        E("vector", lambda e, bufi=bufi, t=t: ts_(
            e, xln2.ap(), xld[bufi].ap(),
            ln2m.ap()[:, t:t + 1], ln2r.ap()[:, t:t + 1],
            ALU.subtract, ALU.mult),
          wait=[(s_x2r[bufi], xc), (s_ln, P1_LN), (s_qv, xq_gp6[0]),
                (s_w2, W2C)],
          inc=(s_ap2, 1))
        apc = s_ap2.count
        kq, xq_gp6[0] = gen_quant_dve(xln2.ap(), H, xq_sb.ap(),
                                      pre_wait=[(s_ap2, apc)],
                                      out_wait=[(s_t2, s_t2.count)])
        q2_flush()
        qn = q2_st["out_cnt"][kq]
        for k in range(FC):
            E("tensor", lambda e, k=k: e.transpose(
                psT_bf[:, k * 128:(k + 1) * 128],
                xq_sb.ap()[:, k * 128:(k + 1) * 128], ident_sb.ap()),
              wait=([(s_qv, qn)] + b.ps_wait(PST)) if k == 0 else [],
              inc=((s_t2, 1) if k == FC - 1 else None))
        tc = s_t2.count
        E("scalar", lambda e: e.copy(xqT.ap(), psT_bf[:, 0:1024]),
          wait=[(s_t2, tc), (s_gu, s_gu.count)], inc=(s_te2, 1))
        b.ps_read(PST, s_te2, s_te2.count)
        return s_te2.count

    def stageB(t, xqtc):
        hm_ic = []
        for ic in range(6):
            icw, ico = IC_SIZES[ic], IC_OFF[ic]
            par = ic % 2
            psg, psgn = (psX1, "psX1") if par == 0 else (psn["psS10"],
                                                        "psS10")
            psu, psun = ((psn["psS00"], "psS00") if par == 0
                         else (psn["psS11"], "psS11"))
            for (psx, psxn, wsb) in ((psg, psgn, w2g), (psu, psun, w2u)):
                for fc in range(FC):
                    E("tensor", lambda e, psx=psx, wsb=wsb, fc=fc, ico=ico,
                      icw=icw: e.matmul(
                        psx[:, 0:icw],
                        xqT.ap().rearrange("p (c t) -> p c t", c=FC)
                        [:, fc, :],
                        wsb.ap().rearrange("p (c n) -> p c n", c=FC)
                        [:, fc, ico:ico + icw],
                        start=(fc == 0), stop=(fc == FC - 1),
                        skip_group_check=True),
                      wait=([(s_te2, xqtc)] + b.ps_wait(psxn))
                      if fc == 0 else [],
                      inc=((s_gu, 1) if fc == FC - 1 else None))
            guc = s_gu.count
            E("scalar", lambda e, icw=icw, psg=psg, par=par: e.activation(
                sg_sb[par].ap()[:, 0:icw], psg[:, 0:icw], AF.Silu),
              wait=[(s_gu, guc), (s_hm, max(0, s_hm.count - 1))],
              inc=(s_sil, 1))
            b.ps_read(psgn, s_sil, s_sil.count)
            slc = s_sil.count
            E("vector", lambda e, icw=icw, ico=ico, psu=psu, par=par:
              e.tensor_tensor(
                hh.ap()[:, ico:ico + icw], sg_sb[par].ap()[:, 0:icw],
                psu[:, 0:icw], ALU.mult),
              wait=[(s_sil, slc)] + ([(s_qv, qp_hh[0])] if ic == 0 else []),
              inc=(s_hm, 1))
            b.ps_read(psun, s_hm, s_hm.count)
            hm_ic.append(s_hm.count)
        return hm_ic

    def stageC(t, hm_ic):
        ks = []
        for ci, cc0 in enumerate(range(0, II, CWH)):
            cw = min(CWH, II - cc0)
            hmw = hm_ic[min(2 * ci + 1, 5)]
            kq, _ = gen_quant_dve(hh.ap()[:, cc0:cc0 + cw], cw,
                                  hq.ap()[:, cc0:cc0 + cw],
                                  pre_wait=[(s_hm, hmw)],
                                  out_wait=[(s_t2, s_t2.count)])
            ks.append(kq)
        q2_flush()
        qp_hh[0] = q2_st["v_cnt"][ks[-1]]
        return [q2_st["out_cnt"][kq] for kq in ks]

    def stageD(t, hqns):
        sqi, tt = divmod(t, TT)
        bufi = t % 2
        for grp in range(4):
            k0, k1 = grp * 6, min(grp * 6 + 6, ICC)
            for k in range(k0, k1):
                E("tensor", lambda e, k=k, k0=k0: e.transpose(
                    psT_bf[:, (k - k0) * 128:(k - k0 + 1) * 128],
                    hq.ap()[:, k * 128:(k + 1) * 128], ident_sb.ap()),
                  wait=([(s_qv, hqns[min(grp, 2)])] + b.ps_wait(PST))
                  if k == k0 else [],
                  inc=((s_t2, 1) if k == k1 - 1 else None))
            tc = s_t2.count
            E("scalar", lambda e, k0=k0, k1=k1: e.copy(
                hqT.ap()[:, 0:(k1 - k0) * 128],
                psT_bf[:, 0:(k1 - k0) * 128]),
              wait=[(s_t2, tc)], inc=(s_te2, 1))
            b.ps_read(PST, s_te2, s_te2.count)
            hqt = s_te2.count
            for k in range(k0, k1):
                for ncb in range(2):
                    psDk, psDn = ((psP[0], "psP0") if ncb == 0
                                  else (psP[1], "psP1"))
                    wdn = []
                    if k == k0 and ncb == 0:
                        wdn = [(s_te2, hqt)]
                    if k == 0:
                        wdn = wdn + b.ps_wait(psDn)
                    E("tensor", lambda e, k=k, k0=k0, ncb=ncb, psDk=psDk:
                      e.matmul(
                        psDk[:, :],
                        hqT.ap()[:, (k - k0) * 128:(k - k0 + 1) * 128],
                        w2d.ap().rearrange("p (c n) -> p c n", c=ICC)
                        [:, k, ncb * 512:(ncb + 1) * 512],
                        start=(k == 0), stop=(k == ICC - 1), skip_group_check=True),
                      wait=wdn,
                      inc=((s_dn, 1) if (k == ICC - 1 and ncb == 1) else None))
        dnc = s_dn.count
        ob = x2w[bufi]
        E("vector", lambda e, ob=ob, bufi=bufi: e.tensor_tensor(
            ob.ap()[:, 0:512], psP[0][:, :], xld[bufi].ap()[:, 0:512],
            ALU.add),
          wait=[(s_dn, dnc), (s_ow[bufi], s_ow[bufi].count)], inc=(s_ov, 1))
        E("vector", lambda e, ob=ob, bufi=bufi: e.tensor_tensor(
            ob.ap()[:, 512:1024], psP[1][:, :], xld[bufi].ap()[:, 512:1024],
            ALU.add),
          wait=[(s_ov, s_ov.count)], inc=(s_ov, 1))
        ovc = s_ov.count
        b.ps_read("psP0", s_ov, ovc)
        b.ps_read("psP1", s_ov, ovc)
        E("sync", lambda e, ob=ob, sqi=sqi, tt=tt: e.dma_start(
            out_hbm[sqi, tt * 128:(tt + 1) * 128, :], ob.ap()),
          wait=[(s_ov, ovc)], inc=(s_ow[bufi], 16))

    xqtc = stageA(0)
    for t in range(NGT):
        hmc = stageB(t, xqtc)
        if t + 1 < NGT:
            xqtc = stageA(t + 1)
        hqns = stageC(t, hmc)
        stageD(t, hqns)

    OW0, OW1 = s_ow[0].count, s_ow[1].count
    E("sync", lambda e: e.wait_ge(b.sems["ow0"].h, OW0))
    E("sync", lambda e: e.wait_ge(b.sems["ow1"].h, OW1))

    assert b.unit_cursor_early == len(b.units) == b.unit_cursor_late, \
        "undripped deferred units remain"
    for s in b.sems.values():
        s.h = nc.semaphore(s.name).__enter__()

    block = nc.Block().__enter__()

    def runner(lst):
        def _(eng):
            for fn, waits, inc in lst:
                for (s, v) in waits:
                    if v > 0:
                        eng.wait_ge(s.h, v)
                inst = fn(eng)
                if inc is not None:
                    inst.then_inc(inc[0].h, inc[1])
        return _

    block.sync(runner(b.ops["sync"]))
    block.scalar(runner(b.ops["scalar"]))
    block.vector(runner(b.ops["vector"]))
    block.tensor(runner(b.ops["tensor"]))
    block.gpsimd(runner(b.ops["gpsimd"]))
    return nc


_cache = {}


def _build_masks():
    if "m" not in _cache:
        i = np.arange(S)[:, None]
        j = np.arange(S)[None, :]
        causal = j <= i
        strided = causal & ((i - j < STRIDE) | (j % STRIDE == i % STRIDE))
        fixed = causal & ((j // STRIDE == i // STRIDE) |
                          (j % STRIDE >= STRIDE - 1))
        _cache["m"] = (
            np.ascontiguousarray(strided.T.astype(ml_dtypes.bfloat16)),
            np.ascontiguousarray(fixed.T.astype(ml_dtypes.bfloat16)))
    return _cache["m"]


def kernel(L0_attn_x, L0_attn_Wq, L0_attn_Wk, L0_attn_Wv, L0_attn_Wo,
           L0_ffn_w_gate, L0_ffn_w_up, L0_ffn_w_down):
    x = np.ascontiguousarray(np.asarray(L0_attn_x, dtype=np.float32))
    mS, mF = _build_masks()
    if "nc" not in _cache:
        _cache["nc"] = build_program()
    nc = _cache["nc"]
    common = {
        "wq": np.ascontiguousarray(np.asarray(L0_attn_Wq, np.float32)),
        "wk": np.ascontiguousarray(np.asarray(L0_attn_Wk, np.float32)),
        "wv": np.ascontiguousarray(np.asarray(L0_attn_Wv, np.float32)),
        "wo": np.ascontiguousarray(np.asarray(L0_attn_Wo, np.float32)),
        "wg": np.ascontiguousarray(np.asarray(L0_ffn_w_gate, np.float32)),
        "wu": np.ascontiguousarray(np.asarray(L0_ffn_w_up, np.float32)),
        "wd": np.ascontiguousarray(np.asarray(L0_ffn_w_down, np.float32)),
        "maskS": mS, "maskF": mF,
        "ident": np.eye(128, dtype=ml_dtypes.bfloat16),
    }
    in_maps = []
    for c in range(NCORES):
        m = dict(common)
        m["x"] = np.ascontiguousarray(x[c * SEQ_PER_CORE:(c + 1) * SEQ_PER_CORE])
        in_maps.append(m)
    res = run_bass_kernel_spmd(nc, in_maps, core_ids=list(range(NCORES)))
    out = np.concatenate([r["out"] for r in res.results], axis=0)
    return out.astype(np.float32)


# version 12
# speedup vs baseline: 1.0231x; 1.0015x over previous
import math

import numpy as np
import ml_dtypes

import concourse.bass as bass
import concourse.mybir as mybir
from concourse.bass_utils import run_bass_kernel_spmd

F32 = mybir.dt.float32
BF16 = mybir.dt.bfloat16
U8 = mybir.dt.uint8
ALU = mybir.AluOpType
AX = mybir.AxisListType
AF = mybir.ActivationFunctionType

B = 16
S = 2048
H = 1024
NH = 16
HD = 64
II = 2816
NCORES = 8
SEQ_PER_CORE = B // NCORES
TT = S // 128
FC = H // 128
ICC = II // 128
STRIDE = max(1, int(math.isqrt(S)))
EXPB = 12.0
SCALE = HD ** -0.5

NOLAG = False
RC0 = 1.5 * (2.0 ** 23) * 0.5
RC1 = 1.5 * (2.0 ** 23) * 1.0
RC2 = 1.5 * (2.0 ** 23) * 2.0

QCW = 512
NQC = S // QCW
CW = 1024


class Sem:
    __slots__ = ("name", "count", "h")

    def __init__(self, name):
        self.name = name
        self.count = 0
        self.h = None


def ts_(e, out, in0, s1, s2, op0, op1=None):
    if s2 is None:
        return e.tensor_single_scalar(out, in0, s1, op0)
    return e.tensor_scalar(out, in0, s1, s2, op0, op1)


COMPUTE_ENGS = ("scalar", "vector", "gpsimd")
PE_ENGS = ("tensor", "sync")


class Builder:
    def __init__(self, debug=False):
        self.nc = bass.Bass("TRN2", target_bir_lowering=False)
        self.debug = debug
        self.ops = {"sync": [], "scalar": [], "vector": [], "tensor": [],
                    "gpsimd": []}
        self.units = []
        self.cur_unit = None
        self.unit_cursor_early = 0
        self.unit_cursor_late = 0
        self.sems = {}
        self._vsb = []
        self._off = 0
        self._ovl_base = None
        self._ovl_max = 0
        self.guard = {}

    def sem(self, name):
        if name not in self.sems:
            self.sems[name] = Sem(name)
        return self.sems[name]

    def _alloc(self, name, shape, dtype, align=64):
        per = int(np.prod(shape[1:])) * mybir.dt.size(dtype)
        off = (self._off + align - 1) // align * align
        self._off = off + per
        h = self.nc.virtual_sbuf_tensor(name, list(shape), dtype)
        self._vsb.append((h, off))
        return h

    def alias_sbuf(self, name, shape, dtype, of_handle):
        off = next(o for h, o in self._vsb if h is of_handle)
        h = self.nc.virtual_sbuf_tensor(name, list(shape), dtype)
        self._vsb.append((h, off))
        return h

    def finalize_arena(self):
        self._ovl_max = max(self._ovl_max, self._off)
        size = (self._ovl_max + 63) // 64 * 64
        assert size <= 212800, f"arena too big: {size} B/partition"
        nc = self.nc
        self._backing_cm = nc.sbuf_tensor("arena", [128, size], U8)
        self._backing = self._backing_cm.__enter__()
        base = nc.lookup_mloc(self._backing).addr
        for h, off in self._vsb:
            nc._tensor(h.name, h.shape, h.dtype, type="SB")
            mloc = nc.lookup_mloc(h)
            mloc.allocated = True
            mloc.addr = base + off
        return size

    def E(self, eng, fn, wait=(), inc=None, defer=False, late=False):
        if inc is not None:
            inc[0].count += inc[1]
        rec = (fn, [w for w in wait if w is not None], inc)
        if defer:
            self.cur_unit["late" if late else "early"][eng].append(rec)
        else:
            self.ops[eng].append(rec)

    def open_unit(self):
        self.cur_unit = {"early": {k: [] for k in self.ops},
                         "late": {k: [] for k in self.ops}, "gev": 0}

    def close_unit(self, gev):
        self.cur_unit["gev"] = gev
        self.units.append(self.cur_unit)
        self.cur_unit = None

    def _emit_late(self, k, state):
        u = self.units[k]
        for eng, lst in u["late"].items():
            for fn, waits, inc in lst:
                if eng == "tensor" and state["first_pe"] \
                        and state["pe_wait"] is not None:
                    waits = list(waits) + [state["pe_wait"]]
                    state["first_pe"] = False
                self.ops[eng].append((fn, waits, inc))
        state["last_gev"] = u["gev"]
        self.unit_cursor_late = k + 1

    def drip_units(self, n, pe_wait=None, ps_guard_cb=None):
        e0, e1 = self.unit_cursor_early, min(self.unit_cursor_early + n,
                                             len(self.units))
        state = {"first_pe": True, "pe_wait": pe_wait, "last_gev": None}
        for k in range(e0, e1):
            for eng, lst in self.units[k]["early"].items():
                self.ops[eng].extend(lst)
            for lk in range(self.unit_cursor_late, k - 1):
                self._emit_late(lk, state)
        self.unit_cursor_early = e1
        if state["last_gev"] is not None and ps_guard_cb is not None:
            ps_guard_cb(state["last_gev"])

    def drip_flush(self, pe_wait=None, ps_guard_cb=None):
        state = {"first_pe": True, "pe_wait": pe_wait, "last_gev": None}
        for k in range(self.unit_cursor_early, len(self.units)):
            for eng, lst in self.units[k]["early"].items():
                self.ops[eng].extend(lst)
            for lk in range(self.unit_cursor_late, k - 1):
                self._emit_late(lk, state)
        self.unit_cursor_early = len(self.units)
        for lk in range(self.unit_cursor_late, len(self.units)):
            self._emit_late(lk, state)
        if state["last_gev"] is not None and ps_guard_cb is not None:
            ps_guard_cb(state["last_gev"])

    def dma(self, eng, out_ap, in_ap, semm, wait=()):
        self.E(eng, lambda e, o=out_ap, i=in_ap: e.dma_start(o, i), wait=wait,
               inc=(semm, 16))
        return semm.count

    def ps_wait(self, name):
        w = self.guard.get(name, [])
        self.guard[name] = []
        return w

    def ps_read(self, name, semm, count):
        self.guard.setdefault(name, []).append((semm, count))


def build_program(debug=False):
    b = Builder(debug)
    nc = b.nc
    E, sem = b.E, b.sem

    x_in = nc.dram_tensor("x", [SEQ_PER_CORE, S, H], F32, kind="ExternalInput").ap()
    w_in = {}
    for nm, shp in (("wq", [H, H]), ("wk", [H, H]), ("wv", [H, H]), ("wo", [H, H]),
                    ("wg", [II, H]), ("wu", [II, H]), ("wd", [H, II])):
        w_in[nm] = nc.dram_tensor(nm, shp, F32, kind="ExternalInput").ap()
    maskS = nc.dram_tensor("maskS", [S, S], BF16, kind="ExternalInput").ap()
    maskF = nc.dram_tensor("maskF", [S, S], BF16, kind="ExternalInput").ap()
    ident_in = nc.dram_tensor("ident", [128, 128], BF16, kind="ExternalInput").ap()
    out_hbm = nc.dram_tensor("out", [SEQ_PER_CORE, S, H], F32,
                             kind="ExternalOutput").ap()
    v_hbm = nc.dram_tensor("vh", [S, NH * 65], BF16, kind="Internal").ap()
    x2s = nc.dram_tensor("x2s", [SEQ_PER_CORE, S, H], F32,
                         kind="ExternalOutput" if debug else "Internal").ap()
    wT = {}
    for nm, shp in (("wqT", [H, H]), ("wkT", [H, H]), ("wvT", [H, H]),
                    ("woT", [H, H]), ("wgT", [H, II]), ("wuT", [H, II]),
                    ("wdT", [II, H])):
        wT[nm] = nc.dram_tensor(nm + "s", shp, BF16, kind="Internal").ap()
    dbg = {}
    if debug:
        for nm, shp, dt_ in (("d_bigT", [128, FC * S], BF16),
                             ("d_atok", [128, TT * H], BF16)):
            dbg[nm] = nc.dram_tensor(nm, shp, dt_, kind="ExternalOutput").ap()

    A = b._alloc
    ident_sb = A("ident_sb", [128, 128], BF16)
    consts = A("consts", [128, 8], F32)
    lnsm = A("lnsm", [128, 16], F32)
    ln2m = A("ln2m", [128, SEQ_PER_CORE * TT], F32)
    ln2r = A("ln2r", [128, SEQ_PER_CORE * TT], F32)
    qs_sm = [[A(f"qs_{nm}{i}", [128, 64], F32)
              for nm in ("bm", "bmp", "inv", "i6", "s2")] for i in range(2)]
    xld = [A(f"xld{i}", [128, H], F32) for i in range(2)]
    x2w = [A(f"x2w{i}", [128, H], F32) for i in range(2)]
    pvr = A("pvr", [128, 16], F32)
    zbf = A("zbf", [128, 272], BF16)

    b._ovl_base = b._off
    g_f = [A(f"g_f{i}", [128, CW], F32) for i in range(6)]
    g_out = [A(f"g_out{i}", [128, CW], BF16) for i in range(2)]
    g_sm = [[A(f"g_{nm}{i}", [128, CW // 16], F32)
             for nm in ("bm", "bmp", "inv", "i6", "s2")] for i in range(2)]
    g_st = A("g_st", [128, CW], F32)
    g_s2f = A("g_s2f", [128, CW], F32)
    g_stbf = b.alias_sbuf("g_stbf", [128, 2 * CW], BF16, g_st)
    g_ev = [A(f"g_ev{i}", [128, 1024], BF16) for i in range(2)]

    bigT = A("bigT", [128, FC * S], BF16)
    atok = A("atok", [128, TT * H], BF16)
    vaug = A("vaug", [128, TT * NH * 65], BF16)
    vsum = A("vsum", [128, NH * 65], BF16)
    qtp = A("qtp", [128, S], BF16)
    ktp = A("ktp", [128, S], BF16)
    wslq = [A(f"wslq{i}", [128, FC * 128], BF16) for i in range(2)]
    wslk = [A(f"wslk{i}", [128, FC * 128], BF16) for i in range(2)]
    wfull = A("wfull", [128, FC * H], BF16)
    ptr_ = [A(f"ptr{i}", [128, QCW], BF16) for i in range(2)]
    ptm = [A(f"ptm{i}", [128, QCW], BF16) for i in range(4)]
    mbuf = [A(f"mbuf{i}", [128, QCW], BF16) for i in range(2)]
    xlnb = [A(f"xlnb{i}", [128, H], BF16) for i in range(2)]
    dum_act = b.alias_sbuf("dum_act", [128, H], F32, ptr_[0])
    p1_end = b._off

    b._ovl_max = max(b._ovl_max, b._off)
    b._off = b._ovl_base
    w2g = A("w2g", [128, FC * II], BF16)
    w2u = A("w2u", [128, FC * II], BF16)
    w2d = A("w2d", [128, ICC * H], BF16)
    CWH = 1024
    p2_f = [A(f"p2_f{i}", [128, CWH], F32) for i in range(6)]
    hh = A("hh", [128, II], F32)
    hq = A("hq", [128, II], BF16)
    hqT = A("hqT", [128, 768], BF16)
    xq_sb = A("xq_sb", [128, H], BF16)
    xqT = [A(f"xqT{i}", [128, H], BF16) for i in range(2)]
    xln2 = A("xln2", [128, H], F32)
    sg_sb = [A(f"sg_sb{i}", [128, 512], F32) for i in range(2)]

    arena_size = b.finalize_arena()

    psn = {}
    for nm in ("psS00", "psS01", "psS10", "psS11", "psP0", "psP1",
               "psX0", "psX1"):
        psn[nm] = nc.psum_tensor(nm, [128, 512], F32).__enter__()
    psS = [[psn["psS00"], psn["psS01"]], [psn["psS10"], psn["psS11"]]]
    psSn = [["psS00", "psS01"], ["psS10", "psS11"]]
    psP = [psn["psP0"], psn["psP1"]]
    psX0, psX1 = psn["psX0"], psn["psX1"]
    tr_h = nc.virtual_psum_tensor("psTbf", [128, 1024], BF16)
    nc._tensor(tr_h.name, tr_h.shape, tr_h.dtype, type="PSUM")
    _m = nc.lookup_mloc(tr_h)
    _m.allocated = True
    _m.addr = nc.lookup_mloc(psX0).addr
    _m.bank = nc.lookup_mloc(psX0).bank
    psT_bf = tr_h.ap()
    PST = "psX0"
    tr2_h = nc.virtual_psum_tensor("psT2bf", [128, 1024], BF16)
    nc._tensor(tr2_h.name, tr2_h.shape, tr2_h.dtype, type="PSUM")
    _m2 = nc.lookup_mloc(tr2_h)
    _m2.allocated = True
    _m2.addr = nc.lookup_mloc(psn["psP0"]).addr
    _m2.bank = nc.lookup_mloc(psn["psP0"]).bank
    psT2_bf = tr2_h.ap()

    s_qv = sem("qv")
    s_qa = sem("qa")
    s_qp = sem("qp")
    s_gv = sem("gv")
    s_gp = sem("gp")
    s_ga = sem("ga")

    def make_quant(F, SM, s_v, s_p, defer):
        st = {"k": 0, "gp19": []}

        def gen(src_ap, n, out_ap, pre_wait=(), out_wait=()):
            k = st["k"]
            st["k"] += 1
            nb = n // 16
            f = [h.ap()[:, 0:n] for h in F]
            bm, bmp, inv, i6, s2 = (h.ap()[:, 0:nb] for h in SM[k % 2])

            def sv(fn, extra=()):
                E("vector", fn, wait=[(s_v, s_v.count)] + list(extra),
                  inc=(s_v, 1), defer=defer)

            def sp(fn, extra=()):
                E("gpsimd", fn, wait=[(s_p, s_p.count)] + list(extra),
                  inc=(s_p, 1), defer=defer)

            w_sm = [(s_p, st["gp19"][k - 2])] if k >= 2 else []
            sv(lambda e: e.tensor_reduce(
                bm, src_ap.rearrange("p (bb kk) -> p bb kk", kk=16),
                AX.X, ALU.max, apply_absolute_value=True),
               extra=list(pre_wait) + w_sm)
            sv(lambda e: ts_(e, bmp, bm, 6e-12, None, ALU.max))
            sv(lambda e: e.reciprocal(inv, bmp))
            sv(lambda e: ts_(e, i6, inv, 6.0, None, ALU.mult))
            sv(lambda e: ts_(e, s2, bmp, 1.0 / 6.0, None, ALU.mult))
            gv5 = s_v.count
            sp(lambda e, nb_=nb: e.tensor_tensor(
                f[0].rearrange("p (bb kk) -> p kk bb", bb=nb_),
                src_ap.rearrange("p (bb kk) -> p kk bb", bb=nb_),
                i6.rearrange("p (o bb) -> p o bb", o=1)
                .broadcast_to([128, 16, nb_]), ALU.mult),
               extra=[(s_v, gv5)])
            gp6 = s_p.count
            sv(lambda e: ts_(e, f[1], f[0], RC0, RC0, ALU.add, ALU.subtract),
               extra=[(s_p, gp6)])
            sv(lambda e: ts_(e, f[2], f[0], RC1, RC1, ALU.add, ALU.subtract))
            sv(lambda e: ts_(e, f[3], f[0], RC2, RC2, ALU.add, ALU.subtract))
            sv(lambda e: ts_(e, f[0], f[1], -2.0, 2.0, ALU.max, ALU.min))
            sv(lambda e: ts_(e, f[1], f[2], -4.0, 4.0, ALU.max, ALU.min))
            sv(lambda e: ts_(e, f[4], f[2], -2.0, 2.0, ALU.max, ALU.min))
            sv(lambda e: ts_(e, f[2], f[3], -6.0, 6.0, ALU.max, ALU.min))
            sv(lambda e: ts_(e, f[5], f[3], -4.0, 4.0, ALU.max, ALU.min))
            gv14 = s_v.count
            sp(lambda e: e.tensor_tensor(f[3], f[1], f[4], ALU.subtract),
               extra=[(s_v, gv14)])
            sp(lambda e: e.tensor_tensor(f[1], f[2], f[5], ALU.subtract))
            sp(lambda e: e.tensor_tensor(f[4], f[3], f[1], ALU.add))
            sp(lambda e: e.tensor_tensor(f[2], f[4], f[0], ALU.add))
            sp(lambda e, nb_=nb: e.tensor_tensor(
                out_ap.rearrange("p (bb kk) -> p kk bb", bb=nb_),
                f[2].rearrange("p (bb kk) -> p kk bb", bb=nb_),
                s2.rearrange("p (o bb) -> p o bb", o=1)
                .broadcast_to([128, 16, nb_]), ALU.mult),
               extra=list(out_wait))
            st["gp19"].append(s_p.count)
            return s_p.count, gp6
        return gen


    def make_quant_dve(F, SM, s_v, s_a, S2F=None, defer=False,
                       act_i6=True):
        st = {"k": 0, "v_cnt": [], "out_cnt": [], "t3": 0, "S": 0,
              "out": 0, "a_cnt": []}

        def sv(fn, extra=(), chain=True):
            E("vector", fn,
              wait=([(s_v, s_v.count)] if chain else []) + list(extra),
              inc=(s_v, 1), defer=defer)

        def sa(fn, extra=()):
            E("scalar", fn, wait=list(extra), inc=(s_a, 1), defer=defer)

        def gen(src_ap, n, out_ap, pre_wait=(), out_wait=()):
            k = st["k"]
            st["k"] += 1
            nb = n // 16
            f = [h.ap()[:, 0:n] for h in F]
            bm, bmp, inv, i6, s2 = (h.ap()[:, 0:nb] for h in SM[k % 2])
            w_sm = ([(s_v, st["out_cnt"][k - 2]),
                     (s_a, st["a_cnt"][k - 2])] if k >= 2 else [])
            sv(lambda e: e.tensor_reduce(
                bm, src_ap.rearrange("p (bb kk) -> p bb kk", kk=16),
                AX.X, ALU.max, apply_absolute_value=True),
               extra=list(pre_wait) + w_sm)
            sv(lambda e: ts_(e, bmp, bm, 6e-12, None, ALU.max))
            sv(lambda e: e.reciprocal(inv, bmp))
            sv(lambda e: ts_(e, i6, inv, 6.0, None, ALU.mult))
            sv(lambda e: ts_(e, s2, bmp, 1.0 / 6.0, None, ALU.mult))
            gv5 = s_v.count
            if act_i6:
                sa(lambda e, nb_=nb: e.activation(
                    f[0].rearrange("p (bb kk) -> p kk bb", bb=nb_),
                    i6.rearrange("p (o bb) -> p o bb", o=1)
                    .broadcast_to([128, 16, nb_]), AF.Copy),
                   extra=list(pre_wait) + [(s_v, max(gv5, st["t3"]))])
            ca1 = s_a.count
            if S2F is not None:
                s2f = S2F.ap()[:, 0:n]
                sa(lambda e, nb_=nb: e.activation(
                    s2f.rearrange("p (bb kk) -> p kk bb", bb=nb_),
                    s2.rearrange("p (o bb) -> p o bb", o=1)
                    .broadcast_to([128, 16, nb_]), AF.Copy),
                   extra=[(s_v, max(gv5, st["out"]))])
            ca2 = s_a.count
            if act_i6:
                sv(lambda e: e.tensor_tensor(f[1], src_ap, f[0], ALU.mult),
                   extra=[(s_a, ca1), (s_v, st["S"])], chain=False)
            else:
                sv(lambda e, nb_=nb: e.tensor_tensor(
                    f[1].rearrange("p (bb kk) -> p kk bb", bb=nb_),
                    src_ap.rearrange("p (bb kk) -> p kk bb", bb=nb_),
                    i6.rearrange("p (o bb) -> p o bb", o=1)
                    .broadcast_to([128, 16, nb_]), ALU.mult),
                   extra=[(s_v, max(gv5, st["t3"], st["S"]))], chain=False)
            cv = s_v.count
            sv(lambda e: ts_(e, f[2], f[1], RC0, RC0, ALU.add, ALU.subtract),
               extra=[(s_v, cv)], chain=False)
            cw0 = s_v.count
            sv(lambda e: ts_(e, f[3], f[1], RC1, RC1, ALU.add, ALU.subtract),
               extra=[(s_v, cv)], chain=False)
            cw1 = s_v.count
            sv(lambda e: ts_(e, f[4], f[1], RC2, RC2, ALU.add, ALU.subtract),
               extra=[(s_v, cv)], chain=False)
            cw2 = s_v.count
            sv(lambda e: ts_(e, f[5], f[2], -2.0, 2.0, ALU.max, ALU.min),
               extra=[(s_v, cw0)], chain=False)
            cc0 = s_v.count
            sv(lambda e: ts_(e, f[0], f[3], -4.0, 4.0, ALU.max, ALU.min),
               extra=[(s_v, max(cw1, cv))], chain=False)
            cA1 = s_v.count
            sv(lambda e: ts_(e, f[1], f[3], -2.0, 2.0, ALU.max, ALU.min),
               extra=[(s_v, cw2)], chain=False)
            cB1 = s_v.count
            sv(lambda e: ts_(e, f[2], f[4], -6.0, 6.0, ALU.max, ALU.min),
               extra=[(s_v, cc0)], chain=False)
            cA2 = s_v.count
            sv(lambda e: ts_(e, f[3], f[4], -4.0, 4.0, ALU.max, ALU.min),
               extra=[(s_v, cB1)], chain=False)
            cB2 = s_v.count
            sv(lambda e: e.tensor_tensor(f[4], f[0], f[1], ALU.subtract),
               extra=[(s_v, cB2)], chain=False)
            cu = s_v.count
            sv(lambda e: e.tensor_tensor(f[0], f[2], f[3], ALU.subtract),
               extra=[(s_v, cu)], chain=False)
            ct2 = st["t3"] = s_v.count
            sv(lambda e: e.tensor_tensor(f[1], f[4], f[0], ALU.add),
               extra=[(s_v, ct2)], chain=False)
            ct3 = s_v.count
            sv(lambda e: e.tensor_tensor(f[2], f[1], f[5], ALU.add),
               extra=[(s_v, ct3)], chain=False)
            cS = st["S"] = s_v.count
            if S2F is not None:
                sv(lambda e: e.tensor_tensor(out_ap, f[2], s2f, ALU.mult),
                   extra=[(s_v, cS), (s_a, ca2)] + list(out_wait),
                   chain=False)
            else:
                sv(lambda e, nb_=nb: e.tensor_tensor(
                    out_ap.rearrange("p (bb kk) -> p kk bb", bb=nb_),
                    f[2].rearrange("p (bb kk) -> p kk bb", bb=nb_),
                    s2.rearrange("p (o bb) -> p o bb", o=1)
                    .broadcast_to([128, 16, nb_]), ALU.mult),
                   extra=[(s_v, cS)] + list(out_wait), chain=False)
            st["out"] = s_v.count
            st["v_cnt"].append(cv)
            st["out_cnt"].append(s_v.count)
            st["a_cnt"].append(ca2)
            return k, cv
        return gen, st

    gen_quant_dve, q2_st = make_quant_dve(p2_f, qs_sm, s_qv, s_qa,
                                          act_i6=False)
    gen_quant_drip, gd_st = make_quant_dve(g_f, g_sm, s_gv, s_ga,
                                           S2F=g_s2f, defer=True)


    def q2_flush():
        pass

    s_wld = sem("wld")
    s_p0mm = sem("p0mm")
    s_p0ev = sem("p0ev")
    s_p0st = [sem("p0st0"), sem("p0st1")]
    s_ini = sem("ini")

    E("vector", lambda e: e.memset(consts.ap()[:, 0:1], 1e-5), inc=(s_ini, 1))
    E("vector", lambda e: e.memset(consts.ap()[:, 1:2], -EXPB),
      wait=[(s_ini, s_ini.count)], inc=(s_ini, 1))
    E("vector", lambda e: e.memset(zbf.ap(), 0.0),
      wait=[(s_ini, s_ini.count)], inc=(s_ini, 1))
    INI = s_ini.count
    s_idl = sem("idl")
    b.dma("sync", ident_sb.ap(), ident_in, s_idl)
    IDL = s_idl.count

    p0ev_attn = [A if False else None]
    for nm in ("wq", "wk", "wv", "wo"):
        src, dst = w_in[nm], wT[nm + "T"]
        for ot in range(FC):
            st_bf = g_stbf.ap()[:, 0:1024]
            E("gpsimd", lambda e, o=st_bf, i=src[ot * 128:(ot + 1) * 128, :]:
              e.dma_start(o, i), wait=[(s_p0mm, s_p0mm.count)], inc=(s_wld, 16))
            ld = s_wld.count
            for k in range(FC):
                E("tensor", lambda e, k=k, src_=st_bf: e.transpose(
                    psT_bf[:, k * 128:(k + 1) * 128],
                    src_[:, k * 128:(k + 1) * 128], ident_sb.ap()),
                  wait=([(s_wld, ld), (s_idl, IDL)] + b.ps_wait(PST))
                  if k == 0 else [],
                  inc=((s_p0mm, 1) if k == FC - 1 else None))
            mm = s_p0mm.count
            ev = g_ev[ot % 2]
            E("vector", lambda e, ev=ev: e.tensor_copy(
                ev.ap()[:, 0:1024], psT_bf[:, 0:1024]),
              wait=[(s_p0mm, mm), (s_p0st[ot % 2], s_p0st[ot % 2].count)],
              inc=(s_p0ev, 1))
            b.ps_read(PST, s_p0ev, s_p0ev.count)
            evc = s_p0ev.count
            dst_ap = bass.AP(dst.tensor, ot * 128,
                             [[H, 128], [128 * H, FC], [1, 128]])
            E("sync", lambda e, o=dst_ap, i=ev.ap()[:, 0:1024].rearrange(
                "p (k c) -> p k c", k=FC): e.dma_start(o, i),
              wait=[(s_p0ev, evc)], inc=(s_p0st[ot % 2], 16))
    P0A_ST0 = s_p0st[0].count
    P0A_ST1 = s_p0st[1].count
    P0A_W = [(s_p0st[0], P0A_ST0), (s_p0st[1], P0A_ST1)]

    s_gld = sem("gld")
    s_gmm = sem("gmm")
    s_gev = sem("gev")
    s_gst = [sem("gst0"), sem("gst1")]

    unit_no = [0]
    gp6_prev = [0]
    for nm, rows, cols in (("wg", II, H), ("wu", II, H), ("wd", H, II)):
        src, dst = w_in[nm], wT[nm + "T"]
        for ot in range(rows // 128):
            for cc0 in range(0, cols, CW):
                b.open_unit()
                unit_no[0] += 1
                cw = min(CW, cols - cc0)
                E("scalar", lambda e, o=g_st.ap()[:, 0:cw],
                  i=src[ot * 128:(ot + 1) * 128, cc0:cc0 + cw]:
                  e.dma_start(o, i),
                  wait=[(s_gv, gp6_prev[0])], inc=(s_gld, 16), defer=True)
                ld = s_gld.count
                qdst = g_out[unit_no[0] % 2].ap()[:, 0:cw]
                kq, gp6_prev[0] = gen_quant_drip(
                    g_st.ap()[:, 0:cw], cw, qdst,
                    pre_wait=[(s_gld, ld)],
                    out_wait=[(s_gmm, max(0, s_gmm.count - 1))])
                gqn = gd_st["out_cnt"][kq]
                nck = cw // 128
                for k in range(nck):
                    E("tensor", lambda e, k=k, qdst=qdst: e.transpose(
                        psT2_bf[:, k * 128:(k + 1) * 128],
                        qdst[:, k * 128:(k + 1) * 128], ident_sb.ap()),
                      wait=[(s_gv, gqn), (s_gev, s_gev.count)]
                      if k == 0 else [],
                      inc=((s_gmm, 1) if k == nck - 1 else None),
                      defer=True, late=True)
                mm = s_gmm.count
                ev = g_ev[unit_no[0] % 2]
                E("vector", lambda e, nck=nck, ev=ev: e.tensor_copy(
                    ev.ap()[:, 0:nck * 128], psT2_bf[:, 0:nck * 128]),
                  wait=[(s_gmm, mm),
                        (s_gst[unit_no[0] % 2], s_gst[unit_no[0] % 2].count)],
                  inc=(s_gev, 1), defer=True, late=True)
                evc = s_gev.count
                dst_ap = bass.AP(dst.tensor, cc0 * rows + ot * 128,
                                 [[rows, 128], [128 * rows, nck], [1, 128]])
                E("sync", lambda e, o=dst_ap,
                  i=ev.ap()[:, 0:nck * 128].rearrange("p (k c) -> p k c",
                                                      k=nck):
                  e.dma_start(o, i),
                  wait=[(s_gev, evc)], inc=(s_gst[unit_no[0] % 2], 16),
                  defer=True, late=True)
                b.close_unit(evc)

    import os as _os
    if _os.environ.get("K_NODRIP"):
        b.drip_units(len(b.units))
        b.drip_units(len(b.units))
    P0B_ST0 = s_gst[0].count
    P0B_ST1 = s_gst[1].count
    P0B_GP = s_gp.count
    P0B_GV = s_gv.count
    P0B_GEV = s_gev.count
    P0B_GMM = s_gmm.count

    s_x = [sem("x0"), sem("x1")]
    s_ln = sem("ln")
    s_lnv = sem("lnv")
    s_ap1 = sem("ap1")
    s_xt = sem("xt")
    s_xte = sem("xte")
    s_wf = sem("wf")
    s_vps = sem("vps")
    s_vev = sem("vev")
    s_qkl = [sem("qkl0"), sem("qkl1")]
    s_qkm = sem("qkm")
    s_qke = sem("qke")
    s_sa = sem("sa")
    s_sb = sem("sb")
    s_ea = sem("ea")
    s_eb = sem("eb")
    s_ma = sem("ma")
    s_mb = sem("mb")
    s_pva = sem("pva")
    s_pvb = sem("pvb")
    s_pvn = sem("pvn")
    s_mld = [sem("mld0"), sem("mld1")]
    s_wo = sem("wo")
    s_res = sem("res")
    s_x2w = [sem("x2w0"), sem("x2w1")]

    def ln_stats(xap, m_slot, r_slot, pre_wait):
        su = lnsm.ap()[:, 0:1]
        sq = lnsm.ap()[:, 1:2]
        var = lnsm.ap()[:, 3:4]
        lnv = lnsm.ap()[:, 4:5]
        E("scalar", lambda e: e.activation(dum_act.ap(), xap, AF.Copy,
                                           accum_out=su),
          wait=list(pre_wait) + [(s_lnv, s_lnv.count)], inc=(s_ln, 1))
        E("scalar", lambda e: e.activation(dum_act.ap(), xap, AF.Square,
                                           accum_out=sq),
          wait=[(s_ln, s_ln.count)], inc=(s_ln, 1))
        lc = s_ln.count
        E("vector", lambda e, m=m_slot: ts_(e, m, su, 1.0 / H, None, ALU.mult),
          wait=[(s_ln, lc)], inc=(s_lnv, 1))
        E("vector", lambda e, m=m_slot: e.tensor_tensor(var, m, m, ALU.mult),
          wait=[(s_lnv, s_lnv.count)], inc=(s_lnv, 1))
        E("vector", lambda e: e.scalar_tensor_tensor(var, sq, 1.0 / H, var,
                                                     ALU.mult, ALU.subtract),
          wait=[(s_lnv, s_lnv.count)], inc=(s_lnv, 1))
        vc = s_lnv.count
        E("scalar", lambda e: e.activation(lnv, var, AF.Ln,
                                           bias=consts.ap()[:, 0:1]),
          wait=[(s_lnv, vc), (s_ini, INI)], inc=(s_ln, 1))
        E("scalar", lambda e, r=r_slot: e.activation(r, lnv, AF.Exp,
                                                     scale=-0.5),
          wait=[(s_ln, s_ln.count)], inc=(s_ln, 1))
        return s_ln.count

    def emit_pv(hb, hg, tkind, tv, first, qlo, ktmax, pt_m, wsem, wcnt,
                pv0):
        s_pv = s_pva if hb == 0 else s_pvb
        psPn = "psP%d" % hb
        kt = tv if tkind == "band" else 0
        first_q = max(kt, qlo // 128) if tkind == "band" else qlo // 128
        for qg in range(first_q, ktmax):
            ql = qg - qlo // 128
            slot = ql * 65
            wpv = []
            if qg == first_q:
                wpv = [(wsem, wcnt), (s_vev, VEV[0])]
                if tkind == "sum":
                    wpv.append((s_vsm, VSM[0]))
                if first:
                    wpv += b.ps_wait(psPn)
            if tkind == "band":
                rhs = vaug.ap()[:, kt * NH * 65 + hg * 65:
                                kt * NH * 65 + hg * 65 + 65]
                lhs_np = 128
            else:
                rhs = vsum.ap()[0:tv, hg * 65:hg * 65 + 65]
                lhs_np = tv
            E("tensor", lambda e, pm=pt_m, qg=qg, qlo=qlo, rhs=rhs,
              hb=hb, slot=slot, ql=ql, lhs_np=lhs_np, kt=kt, first=first,
              tkind=tkind: e.matmul(
                psP[hb][:, slot:slot + 65],
                pm.ap()[0:lhs_np, qg * 128 - qlo:qg * 128 - qlo + 128],
                rhs,
                start=(first and ql == 0),
                stop=(tkind == "band" and kt == qg), skip_group_check=True),
              wait=wpv,
              inc=((s_pv, 1) if qg == ktmax - 1 else None))

    NPAIR = NH // 2
    mask_hist = []
    VEV = [0]
    VSM = [0]
    s_vhb = sem("vhb")
    s_vsm = sem("vsm")

    for sqi in range(SEQ_PER_CORE):
        for tt in range(TT):
            bufi = tt % 2
            wpre = [(s_ap1, max(0, s_ap1.count - 1)), (s_res, s_res.count)]
            if sqi == 0 and tt == 0:
                wpre.extend(P0A_W)
            E("sync", lambda e, o=xld[bufi].ap(),
              i=x_in[sqi, tt * 128:(tt + 1) * 128, :]: e.dma_start(o, i),
              wait=wpre, inc=(s_x[bufi], 16))
            xc = s_x[bufi].count
            xap = xld[bufi].ap()
            lc = ln_stats(xap, lnsm.ap()[:, 2:3], lnsm.ap()[:, 7:8],
                          [(s_x[bufi], xc)])
            xb_ = xlnb[bufi].ap()
            E("vector", lambda e, xa=xap, o=xb_: ts_(
                e, o, xa, lnsm.ap()[:, 2:3], lnsm.ap()[:, 7:8],
                ALU.subtract, ALU.mult),
              wait=[(s_ln, lc), (s_xt, max(0, s_xt.count - 1))], inc=(s_ap1, 1))
            ap_c = s_ap1.count
            for k in range(FC):
                E("tensor", lambda e, k=k, i=xb_[:, k * 128:(k + 1) * 128]:
                  e.transpose(psT_bf[:, k * 128:(k + 1) * 128], i,
                              ident_sb.ap()),
                  wait=([(s_ap1, ap_c)] + b.ps_wait(PST)) if k == 0 else [],
                  inc=((s_xt, 1) if k == FC - 1 else None))
            mmc = s_xt.count
            E("vector", lambda e, tt=tt: e.tensor_copy(
                bigT.ap().rearrange("p (c t) -> p c t", c=FC)
                [:, :, tt * 128:(tt + 1) * 128],
                psT_bf[:, 0:1024].rearrange("p (c t) -> p c t", c=FC)),
              wait=[(s_xt, mmc)], inc=(s_xte, 1))
            b.ps_read(PST, s_xte, s_xte.count)
        XTE = s_xte.count
        b.drip_units(2, pe_wait=(s_pvn, s_pvn.count),
                     ps_guard_cb=lambda g: b.ps_read("psP0", s_gev, g))

        E("scalar", lambda e: e.dma_start(
            wfull.ap().rearrange("p (c n) -> p c n", c=FC),
            wT["wvT"].rearrange("(c p) n -> p c n", p=128)),
          wait=P0A_W + [(s_qkm, s_qkm.count), (s_wo, s_wo.count)],
          inc=(s_wf, 16))
        wfc = s_wf.count
        E("vector", lambda e: e.memset(
            vaug.ap().rearrange("p (t h j) -> p t h j", h=NH, j=65)
            [:, :, :, 64:65], 1.0),
          wait=[(s_pvn, s_pvn.count)], inc=(s_vev, 1))
        for tt in range(TT):
            for nch in range(2):
                psV, psVn = (psX1, "psX1") if nch == 0 else (psn["psS00"],
                                                             "psS00")
                for fc in range(FC):
                    E("tensor", lambda e, tt=tt, fc=fc, nch=nch, psV=psV:
                      e.matmul(
                        psV[:, :],
                        bigT.ap().rearrange("p (c t) -> p c t", c=FC)
                        [:, fc, tt * 128:(tt + 1) * 128],
                        wfull.ap().rearrange("p (c n) -> p c n", c=FC)
                        [:, fc, nch * 512:(nch + 1) * 512],
                        start=(fc == 0), stop=(fc == FC - 1), skip_group_check=True),
                      wait=([(s_wf, wfc), (s_xte, XTE)] + b.ps_wait(psVn))
                      if fc == 0 else [],
                      inc=((s_vps, 1) if fc == FC - 1 else None))
                vp = s_vps.count
                E("vector", lambda e, tt=tt, nch=nch, psV=psV: e.tensor_copy(
                    vaug.ap().rearrange("p (t h j) -> p t h j", h=NH, j=65)
                    [:, tt, nch * 8:(nch + 1) * 8, 0:64],
                    psV[:, :].rearrange("p (h j) -> p h j", h=8)),
                  wait=[(s_vps, vp)], inc=(s_vev, 1))
                b.ps_read(psVn, s_vev, s_vev.count)
            vevc = s_vev.count
            E("sync", lambda e, tt=tt: e.dma_start(
                v_hbm[tt * 128:(tt + 1) * 128, :],
                vaug.ap()[:, tt * NH * 65:(tt + 1) * NH * 65]),
              wait=[(s_vev, vevc)] + ([(s_vsm, VSM[0])] if tt == 0 else []),
              inc=(s_vhb, 16))
        VEV[0] = s_vev.count
        E("sync", lambda e: e.dma_start(
            vsum.ap()[0:45, :],
            bass.AP(v_hbm.tensor, 44 * NH * 65,
                    [[45 * NH * 65, 45], [1, NH * 65]])),
          wait=[(s_vhb, s_vhb.count)], inc=(s_vsm, 16))
        VSM[0] = s_vsm.count

        for pair in range(NPAIR):
            mask_t = maskS if pair < NPAIR // 2 else maskF
            pc = pair * 128
            wbq, wbk = wslq[pair % 2], wslk[pair % 2]
            wq_ap = bass.AP(wT["wqT"].tensor, pc, [[H, 128], [128 * H, FC],
                                                   [1, 128]])
            wk_ap = bass.AP(wT["wkT"].tensor, pc, [[H, 128], [128 * H, FC],
                                                   [1, 128]])
            wpre = ([(s_qkm, max(0, s_qkm.count - 16))]
                    if (sqi * NPAIR + pair) >= 2 else list(P0A_W))
            s_qkl_p = s_qkl[pair % 2]
            E("scalar", lambda e, o=wbq.ap().rearrange("p (c n) -> p c n",
                                                       c=FC),
              i=wq_ap: e.dma_start(o, i), wait=wpre, inc=(s_qkl_p, 16))
            E("scalar", lambda e, o=wbk.ap().rearrange("p (c n) -> p c n",
                                                       c=FC),
              i=wk_ap: e.dma_start(o, i), wait=[], inc=(s_qkl_p, 16))
            qkc = s_qkl_p.count
            for (wb, outb) in ((wbq, qtp), (wbk, ktp)):
                for nchk in range(4):
                    psQ, psQn = ((psX1, "psX1") if nchk % 2 == 0
                                 else (psn["psS00"], "psS00"))
                    for fc in range(FC):
                        E("tensor", lambda e, wb=wb, nchk=nchk, fc=fc, psQ=psQ:
                          e.matmul(
                            psQ[:, :],
                            wb.ap().rearrange("p (c n) -> p c n", c=FC)
                            [:, fc, :],
                            bigT.ap().rearrange("p (c t) -> p c t", c=FC)
                            [:, fc, nchk * 512:(nchk + 1) * 512],
                            start=(fc == 0), stop=(fc == FC - 1), skip_group_check=True),
                          wait=([(s_qkl_p, qkc), (s_xte, XTE)] + b.ps_wait(psQn))
                          if fc == 0 else [],
                          inc=((s_qkm, 1) if fc == FC - 1 else None))
                    qm = s_qkm.count
                    E("vector", lambda e, outb=outb, nchk=nchk, psQ=psQ:
                      e.tensor_copy(outb.ap()[:, nchk * 512:(nchk + 1) * 512],
                                    psQ[:, :]),
                      wait=[(s_qkm, qm)], inc=(s_qke, 1))
                    b.ps_read(psQn, s_qke, s_qke.count)
            QKE = s_qke.count

            is_fixed = pair >= NPAIR // 2
            for qc in range(NQC):
                qlo, qhi = qc * QCW, qc * QCW + QCW
                ktmax = qhi // 128
                ktlo = max(0, (qlo - 44) // 128) if is_fixed else 0
                Mc = 0
                if is_fixed and ktlo > 0:
                    Mc = (ktlo * 128 - 45) // 45 + 1
                tiles = ([("sum", Mc)] if Mc > 0 else []) \
                    + [("band", kt) for kt in range(ktlo, ktmax)]
                pva0, pvb0 = s_pva.count, s_pvb.count
                prev = [None, None]
                for idx, (tkind, tv) in enumerate(tiles):
                    if tkind == "band":
                        kt = tv
                        qstart = max(qlo, kt * 128)
                        n = qhi - qstart
                        c0 = qstart - qlo
                        g = len(mask_hist)
                        mb_ = mbuf[g % 2]
                        wm = []
                        if g >= 2:
                            wm = [(s_ma, mask_hist[g - 2][0]),
                                  (s_mb, mask_hist[g - 2][1])]
                        E("sync", lambda e, o=mb_.ap()[:, 0:n],
                          i=mask_t[kt * 128:(kt + 1) * 128, qstart:qhi]:
                          e.dma_start(o, i), wait=wm, inc=(s_mld[g % 2], 16))
                        mldc = s_mld[g % 2].count
                        s_mld_g = s_mld[g % 2]
                        mask_hist.append(None)
                    for hb in (0, 1):
                        hg = pair * 2 + hb
                        psSh = psS[hb][idx % 2]
                        psShn = psSn[hb][idx % 2]
                        s_s = s_sa if hb == 0 else s_sb
                        s_e = s_ea if hb == 0 else s_eb
                        s_m = s_ma if hb == 0 else s_mb
                        s_pv = s_pva if hb == 0 else s_pvb
                        pv0 = pva0 if hb == 0 else pvb0
                        pt_raw = ptr_[hb]
                        pt_m = ptm[2 * hb + (idx % 2)]
                        if tkind == "band":
                            E("tensor", lambda e, kt=kt, hb=hb, c0=c0, n=n,
                              qs=qstart, psSh=psSh: e.matmul(
                                psSh[:, c0:c0 + n],
                                ktp.ap()[64 * hb:64 * hb + 64,
                                         kt * 128:(kt + 1) * 128],
                                qtp.ap()[64 * hb:64 * hb + 64, qs:qs + n],
                                start=True, stop=True, skip_group_check=True),
                              wait=[(s_qke, QKE)] + b.ps_wait(psShn),
                              inc=(s_s, 1))
                        else:
                            sum_ap = bass.AP(ktp.ap().tensor,
                                             (64 * hb) * S + 44,
                                             [[S, 64], [45, Mc]])
                            E("tensor", lambda e, hb=hb, psSh=psSh,
                              sum_ap=sum_ap, qlo=qlo, qhi=qhi, Mc=Mc:
                              e.matmul(
                                psSh[0:Mc, 0:QCW],
                                sum_ap,
                                qtp.ap()[64 * hb:64 * hb + 64, qlo:qhi],
                                start=True, stop=True, skip_group_check=True),
                              wait=[(s_qke, QKE)] + b.ps_wait(psShn),
                              inc=(s_s, 1))
                        sc = s_s.count
                        if prev[hb] is not None:
                            emit_pv(*prev[hb])
                        if tkind == "band":
                            E("scalar", lambda e, psSh=psSh, c0=c0, n=n,
                              pt=pt_raw: e.activation(
                                pt.ap()[:, c0:c0 + n], psSh[:, c0:c0 + n],
                                AF.Exp,
                                bias=consts.ap()[:, 1:2], scale=SCALE),
                              wait=[(s_s, sc), (s_m, s_m.count),
                                    (s_ini, INI)],
                              inc=(s_e, 1))
                            b.ps_read(psShn, s_e, s_e.count)
                            ec = s_e.count
                            wmm = [(s_e, ec), (s_mld_g, mldc)]
                            if idx >= 2:
                                wmm.append((s_pv, pv0 + idx - 1))
                            E("vector", lambda e, pt=pt_raw, pm=pt_m,
                              mb2=mb_, c0=c0, n=n: e.tensor_tensor(
                                pm.ap()[:, c0:c0 + n], pt.ap()[:, c0:c0 + n],
                                mb2.ap()[:, 0:n], ALU.mult),
                              wait=wmm, inc=(s_m, 1))
                            mc = s_m.count
                            prev[hb] = (hb, hg, "band", kt, idx == 0, qlo,
                                        ktmax, pt_m, s_m, mc, pv0)
                            mask_hist[g] = (s_ma.count, s_mb.count)
                        else:
                            drain = sc + 1
                            E("scalar", lambda e, psSh=psSh, pm=pt_m,
                              Mc=Mc: e.activation(
                                pm.ap()[0:Mc, 0:QCW], psSh[0:Mc, 0:QCW],
                                AF.Exp,
                                bias=consts.ap()[0:Mc, 1:2], scale=SCALE),
                              wait=[(s_s, drain), (s_m, s_m.count),
                                    (s_ini, INI)],
                              inc=(s_e, 1))
                            b.ps_read(psShn, s_e, s_e.count)
                            ec = s_e.count
                            prev[hb] = (hb, hg, "sum", Mc, idx == 0, qlo,
                                        ktmax, pt_m, s_e, ec, pv0)
                for hb in (0, 1):
                    if prev[hb] is not None:
                        emit_pv(*prev[hb])
                        prev[hb] = None
                for hb, s_pv in ((0, s_pva), (1, s_pvb)):
                    E("tensor", lambda e, hb=hb: e.matmul(
                        psP[hb][:, 0:260], zbf.ap()[:, 0:128],
                        zbf.ap()[:, 0:260], start=False, stop=True, skip_group_check=True),
                      wait=[(s_ini, INI)], inc=(s_pv, 1))
                for hb in (0, 1):
                    hg = pair * 2 + hb
                    s_pv = s_pva if hb == 0 else s_pvb
                    pvc = s_pv.count
                    for ql in range(QCW // 128):
                        qg = qlo // 128 + ql
                        slot = ql * 65
                        rr = pvr.ap()[:, hb * 8 + ql:hb * 8 + ql + 1]
                        w1_ = [(s_pvn, s_pvn.count)]
                        if ql == 0:
                            w1_.append((s_pv, pvc))
                        E("vector", lambda e, hb=hb, slot=slot, rr=rr:
                          e.reciprocal(rr, psP[hb][:, slot + 64:slot + 65]),
                          wait=w1_, inc=(s_pvn, 1))
                        E("vector", lambda e, hb=hb, slot=slot, rr=rr, qg=qg,
                          hg=hg: ts_(
                            e, atok.ap().rearrange("p (t c) -> p t c", t=TT)
                            [:, qg, hg * 64:(hg + 1) * 64],
                            psP[hb][:, slot:slot + 64], rr, None, ALU.mult),
                          wait=[(s_pvn, s_pvn.count)], inc=(s_pvn, 1))
                        b.ps_read("psP%d" % hb, s_pvn, s_pvn.count)
            b.drip_units(4, pe_wait=(s_pvn, s_pvn.count),
                         ps_guard_cb=lambda g: b.ps_read("psP0", s_gev, g))
        ATOK = s_pvn.count

        for tt in range(TT):
            for k in range(FC):
                E("tensor", lambda e, tt=tt, k=k: e.transpose(
                    psT_bf[:, k * 128:(k + 1) * 128],
                    atok.ap()[:, tt * 1024 + k * 128:tt * 1024 + (k + 1) * 128],
                    ident_sb.ap()),
                  wait=([(s_pvn, ATOK)] + b.ps_wait(PST)) if k == 0 else [],
                  inc=((s_xt, 1) if k == FC - 1 else None))
            atc = s_xt.count
            E("vector", lambda e, tt=tt: e.tensor_copy(
                bigT.ap().rearrange("p (c t) -> p c t", c=FC)
                [:, :, tt * 128:(tt + 1) * 128],
                psT_bf[:, 0:1024].rearrange("p (c t) -> p c t", c=FC)),
              wait=[(s_xt, atc)], inc=(s_xte, 1))
            b.ps_read(PST, s_xte, s_xte.count)
        ATE = s_xte.count

        E("scalar", lambda e: e.dma_start(
            wfull.ap().rearrange("p (c n) -> p c n", c=FC),
            wT["woT"].rearrange("(c p) n -> p c n", p=128)),
          wait=[(s_vps, s_vps.count), (s_qkm, s_qkm.count)], inc=(s_wf, 16))
        wfc = s_wf.count
        for tt in range(TT):
            bufi = tt % 2
            E("sync", lambda e, o=xld[bufi].ap(),
              i=x_in[sqi, tt * 128:(tt + 1) * 128, :]: e.dma_start(o, i),
              wait=[(s_ap1, s_ap1.count), (s_res, max(0, s_res.count - 2))],
              inc=(s_x[bufi], 16))
            xc = s_x[bufi].count
            for ncb in range(2):
                psW, psWn = ((psX1, "psX1") if ncb == 0
                             else (psn["psS00"], "psS00"))
                for fc in range(FC):
                    E("tensor", lambda e, tt=tt, fc=fc, ncb=ncb, psW=psW:
                      e.matmul(
                        psW[:, :],
                        bigT.ap().rearrange("p (c t) -> p c t", c=FC)
                        [:, fc, tt * 128:(tt + 1) * 128],
                        wfull.ap().rearrange("p (c n) -> p c n", c=FC)
                        [:, fc, ncb * 512:(ncb + 1) * 512],
                        start=(fc == 0), stop=(fc == FC - 1), skip_group_check=True),
                      wait=([(s_wf, wfc), (s_xte, ATE)] + b.ps_wait(psWn))
                      if fc == 0 else [],
                      inc=((s_wo, 1) if fc == FC - 1 else None))
            woc = s_wo.count
            x2b = x2w[bufi]
            E("vector", lambda e, x2b=x2b, bufi=bufi: e.tensor_tensor(
                x2b.ap()[:, 0:512], psX1[:, :], xld[bufi].ap()[:, 0:512],
                ALU.add),
              wait=[(s_wo, woc), (s_x[bufi], xc),
                    (s_x2w[bufi], s_x2w[bufi].count)],
              inc=(s_res, 1))
            E("vector", lambda e, x2b=x2b, bufi=bufi: e.tensor_tensor(
                x2b.ap()[:, 512:1024], psn["psS00"][:, :],
                xld[bufi].ap()[:, 512:1024], ALU.add),
              wait=[(s_res, s_res.count)], inc=(s_res, 1))
            rc = s_res.count
            b.ps_read("psX1", s_res, rc)
            b.ps_read("psS00", s_res, rc)
            gt = sqi * TT + tt
            lc = ln_stats(x2b.ap(), ln2m.ap()[:, gt:gt + 1],
                          ln2r.ap()[:, gt:gt + 1], [(s_res, rc)])
            E("sync", lambda e, x2b=x2b, sqi=sqi, tt=tt: e.dma_start(
                x2s[sqi, tt * 128:(tt + 1) * 128, :], x2b.ap()),
              wait=[(s_ln, lc)], inc=(s_x2w[bufi], 16))
        b.drip_units(3, pe_wait=(s_pvn, s_pvn.count),
                     ps_guard_cb=lambda g: b.ps_read("psP0", s_gev, g))
        if debug and sqi == 0:
            b.dma("sync", dbg["d_bigT"], bigT.ap(), sem("dbgs"),
                  wait=[(s_xte, ATE)])
            b.dma("sync", dbg["d_atok"], atok.ap(), sem("dbgs"),
                  wait=[(s_pvn, ATOK)])

    P1_X2W0 = s_x2w[0].count
    P1_X2W1 = s_x2w[1].count
    X2W_W = [(s_x2w[0], P1_X2W0), (s_x2w[1], P1_X2W1)]
    P1_LN = s_ln.count
    b.drip_flush(pe_wait=(s_pvn, s_pvn.count),
                 ps_guard_cb=lambda g: b.ps_read("psP0", s_gev, g))

    s_w2 = sem("w2")
    s_x2r = [sem("x2r0"), sem("x2r1")]
    s_t2 = sem("t2")
    s_te2 = sem("te2")
    s_gu = sem("gu")
    s_sil = sem("sil")
    s_hm = sem("hm")
    s_dn = sem("dn")
    s_ov = sem("ov")
    s_ow = [sem("ow0"), sem("ow1")]
    s_ap2 = sem("ap2")

    E("scalar", lambda e: e.dma_start(
        w2g.ap().rearrange("p (c n) -> p c n", c=FC),
        wT["wgT"].rearrange("(c p) n -> p c n", p=128)),
      wait=X2W_W + [(s_qkm, s_qkm.count), (s_wo, s_wo.count),
            (s_pvn, s_pvn.count), (s_xte, s_xte.count), (s_ma, s_ma.count),
            (s_mb, s_mb.count), (s_gst[0], P0B_ST0), (s_gst[1], P0B_ST1),
            (s_gv, P0B_GV), (s_gev, P0B_GEV), (s_ga, s_ga.count)],
      inc=(s_w2, 16))
    E("scalar", lambda e: e.dma_start(
        w2u.ap().rearrange("p (c n) -> p c n", c=FC),
        wT["wuT"].rearrange("(c p) n -> p c n", p=128)),
      wait=[], inc=(s_w2, 16))
    E("scalar", lambda e: e.dma_start(
        w2d.ap().rearrange("p (c n) -> p c n", c=ICC),
        wT["wdT"].rearrange("(c p) n -> p c n", p=128)),
      wait=[], inc=(s_w2, 16))
    W2C = s_w2.count

    qp_hh = [0]
    xq_gp6 = [0]
    xq_gu = [0]
    IC_SIZES = [512, 512, 512, 512, 512, 256]
    IC_OFF = [0, 512, 1024, 1536, 2048, 2560]
    NGT = SEQ_PER_CORE * TT

    def stageA_dve(t):
        sqi, tt = divmod(t, TT)
        bufi = t % 2
        E("sync", lambda e, o=xld[bufi].ap(),
          i=x2s[sqi, tt * 128:(tt + 1) * 128, :]: e.dma_start(o, i),
          wait=X2W_W + [(s_ap2, max(0, s_ap2.count - 1)),
                        (s_ov, s_ov.count)],
          inc=(s_x2r[bufi], 16))
        xc = s_x2r[bufi].count
        E("vector", lambda e, bufi=bufi, t=t: ts_(
            e, xln2.ap(), xld[bufi].ap(),
            ln2m.ap()[:, t:t + 1], ln2r.ap()[:, t:t + 1],
            ALU.subtract, ALU.mult),
          wait=[(s_x2r[bufi], xc), (s_ln, P1_LN), (s_qv, xq_gp6[0]),
                (s_w2, W2C)],
          inc=(s_ap2, 1))
        apc = s_ap2.count
        kq, xq_gp6[0] = gen_quant_dve(xln2.ap(), H, xq_sb.ap(),
                                      pre_wait=[(s_ap2, apc)],
                                      out_wait=[(s_t2, s_t2.count)])
        return q2_st["out_cnt"][kq]

    def stageA_pe(t, qn):
        for k in range(FC):
            E("tensor", lambda e, k=k: e.transpose(
                psT_bf[:, k * 128:(k + 1) * 128],
                xq_sb.ap()[:, k * 128:(k + 1) * 128], ident_sb.ap()),
              wait=([(s_qv, qn)] + b.ps_wait(PST)) if k == 0 else [],
              inc=((s_t2, 1) if k == FC - 1 else None))
        tc = s_t2.count
        E("scalar", lambda e, t=t: e.copy(xqT[t % 2].ap(),
                                          psT_bf[:, 0:1024]),
          wait=[(s_t2, tc), (s_gu, xq_gu[0])], inc=(s_te2, 1))
        xq_gu[0] = s_gu.count
        b.ps_read(PST, s_te2, s_te2.count)
        return s_te2.count

    def stageB(t, xqtc):
        hm_ic = []
        for ic in range(6):
            icw, ico = IC_SIZES[ic], IC_OFF[ic]
            par = ic % 2
            psg, psgn = (psX1, "psX1") if par == 0 else (psn["psS10"],
                                                        "psS10")
            psu, psun = ((psn["psS00"], "psS00") if par == 0
                         else (psn["psS11"], "psS11"))
            for (psx, psxn, wsb) in ((psg, psgn, w2g), (psu, psun, w2u)):
                for fc in range(FC):
                    E("tensor", lambda e, psx=psx, wsb=wsb, fc=fc, ico=ico,
                      icw=icw: e.matmul(
                        psx[:, 0:icw],
                        xqT[t % 2].ap().rearrange("p (c t) -> p c t", c=FC)
                        [:, fc, :],
                        wsb.ap().rearrange("p (c n) -> p c n", c=FC)
                        [:, fc, ico:ico + icw],
                        start=(fc == 0), stop=(fc == FC - 1),
                        skip_group_check=True),
                      wait=([(s_te2, xqtc)] + b.ps_wait(psxn))
                      if fc == 0 else [],
                      inc=((s_gu, 1) if fc == FC - 1 else None))
            guc = s_gu.count
            E("scalar", lambda e, icw=icw, psg=psg, par=par: e.activation(
                sg_sb[par].ap()[:, 0:icw], psg[:, 0:icw], AF.Silu),
              wait=[(s_gu, guc), (s_hm, max(0, s_hm.count - 1))],
              inc=(s_sil, 1))
            b.ps_read(psgn, s_sil, s_sil.count)
            slc = s_sil.count
            E("vector", lambda e, icw=icw, ico=ico, psu=psu, par=par:
              e.tensor_tensor(
                hh.ap()[:, ico:ico + icw], sg_sb[par].ap()[:, 0:icw],
                psu[:, 0:icw], ALU.mult),
              wait=[(s_sil, slc)] + ([(s_qv, qp_hh[0])] if ic == 0 else []),
              inc=(s_hm, 1))
            b.ps_read(psun, s_hm, s_hm.count)
            hm_ic.append(s_hm.count)
        return hm_ic

    def stageC(t, hm_ic):
        ks = []
        for ci, cc0 in enumerate(range(0, II, CWH)):
            cw = min(CWH, II - cc0)
            hmw = hm_ic[min(2 * ci + 1, 5)]
            kq, _ = gen_quant_dve(hh.ap()[:, cc0:cc0 + cw], cw,
                                  hq.ap()[:, cc0:cc0 + cw],
                                  pre_wait=[(s_hm, hmw)],
                                  out_wait=[(s_t2, s_t2.count)])
            ks.append(kq)
        q2_flush()
        qp_hh[0] = q2_st["v_cnt"][ks[-1]]
        return [q2_st["out_cnt"][kq] for kq in ks]

    def stageD(t, hqns):
        sqi, tt = divmod(t, TT)
        bufi = t % 2
        for grp in range(4):
            k0, k1 = grp * 6, min(grp * 6 + 6, ICC)
            for k in range(k0, k1):
                E("tensor", lambda e, k=k, k0=k0: e.transpose(
                    psT_bf[:, (k - k0) * 128:(k - k0 + 1) * 128],
                    hq.ap()[:, k * 128:(k + 1) * 128], ident_sb.ap()),
                  wait=([(s_qv, hqns[min(grp, 2)])] + b.ps_wait(PST))
                  if k == k0 else [],
                  inc=((s_t2, 1) if k == k1 - 1 else None))
            tc = s_t2.count
            E("scalar", lambda e, k0=k0, k1=k1: e.copy(
                hqT.ap()[:, 0:(k1 - k0) * 128],
                psT_bf[:, 0:(k1 - k0) * 128]),
              wait=[(s_t2, tc)], inc=(s_te2, 1))
            b.ps_read(PST, s_te2, s_te2.count)
            hqt = s_te2.count
            for k in range(k0, k1):
                for ncb in range(2):
                    psDk, psDn = ((psP[0], "psP0") if ncb == 0
                                  else (psP[1], "psP1"))
                    wdn = []
                    if k == k0 and ncb == 0:
                        wdn = [(s_te2, hqt)]
                    if k == 0:
                        wdn = wdn + b.ps_wait(psDn)
                    E("tensor", lambda e, k=k, k0=k0, ncb=ncb, psDk=psDk:
                      e.matmul(
                        psDk[:, :],
                        hqT.ap()[:, (k - k0) * 128:(k - k0 + 1) * 128],
                        w2d.ap().rearrange("p (c n) -> p c n", c=ICC)
                        [:, k, ncb * 512:(ncb + 1) * 512],
                        start=(k == 0), stop=(k == ICC - 1), skip_group_check=True),
                      wait=wdn,
                      inc=((s_dn, 1) if (k == ICC - 1 and ncb == 1) else None))
        dnc = s_dn.count
        ob = x2w[bufi]
        E("vector", lambda e, ob=ob, bufi=bufi: e.tensor_tensor(
            ob.ap()[:, 0:512], psP[0][:, :], xld[bufi].ap()[:, 0:512],
            ALU.add),
          wait=[(s_dn, dnc), (s_ow[bufi], s_ow[bufi].count)], inc=(s_ov, 1))
        E("vector", lambda e, ob=ob, bufi=bufi: e.tensor_tensor(
            ob.ap()[:, 512:1024], psP[1][:, :], xld[bufi].ap()[:, 512:1024],
            ALU.add),
          wait=[(s_ov, s_ov.count)], inc=(s_ov, 1))
        ovc = s_ov.count
        b.ps_read("psP0", s_ov, ovc)
        b.ps_read("psP1", s_ov, ovc)
        E("sync", lambda e, ob=ob, sqi=sqi, tt=tt: e.dma_start(
            out_hbm[sqi, tt * 128:(tt + 1) * 128, :], ob.ap()),
          wait=[(s_ov, ovc)], inc=(s_ow[bufi], 16))

    qn0 = stageA_dve(0)
    xqtc = stageA_pe(0, qn0)
    for t in range(NGT):
        if t + 1 < NGT:
            qn1 = stageA_dve(t + 1)
        hmc = stageB(t, xqtc)
        if t + 1 < NGT:
            xqtc = stageA_pe(t + 1, qn1)
        hqns = stageC(t, hmc)
        stageD(t, hqns)

    OW0, OW1 = s_ow[0].count, s_ow[1].count
    E("sync", lambda e: e.wait_ge(b.sems["ow0"].h, OW0))
    E("sync", lambda e: e.wait_ge(b.sems["ow1"].h, OW1))

    assert b.unit_cursor_early == len(b.units) == b.unit_cursor_late, \
        "undripped deferred units remain"
    for s in b.sems.values():
        s.h = nc.semaphore(s.name).__enter__()

    block = nc.Block().__enter__()

    def runner(lst):
        def _(eng):
            for fn, waits, inc in lst:
                for (s, v) in waits:
                    if v > 0:
                        eng.wait_ge(s.h, v)
                inst = fn(eng)
                if inc is not None:
                    inst.then_inc(inc[0].h, inc[1])
        return _

    block.sync(runner(b.ops["sync"]))
    block.scalar(runner(b.ops["scalar"]))
    block.vector(runner(b.ops["vector"]))
    block.tensor(runner(b.ops["tensor"]))
    block.gpsimd(runner(b.ops["gpsimd"]))
    return nc


_cache = {}


def _build_masks():
    if "m" not in _cache:
        i = np.arange(S)[:, None]
        j = np.arange(S)[None, :]
        causal = j <= i
        strided = causal & ((i - j < STRIDE) | (j % STRIDE == i % STRIDE))
        fixed = causal & ((j // STRIDE == i // STRIDE) |
                          (j % STRIDE >= STRIDE - 1))
        _cache["m"] = (
            np.ascontiguousarray(strided.T.astype(ml_dtypes.bfloat16)),
            np.ascontiguousarray(fixed.T.astype(ml_dtypes.bfloat16)))
    return _cache["m"]


def kernel(L0_attn_x, L0_attn_Wq, L0_attn_Wk, L0_attn_Wv, L0_attn_Wo,
           L0_ffn_w_gate, L0_ffn_w_up, L0_ffn_w_down):
    x = np.ascontiguousarray(np.asarray(L0_attn_x, dtype=np.float32))
    mS, mF = _build_masks()
    if "nc" not in _cache:
        _cache["nc"] = build_program()
    nc = _cache["nc"]
    common = {
        "wq": np.ascontiguousarray(np.asarray(L0_attn_Wq, np.float32)),
        "wk": np.ascontiguousarray(np.asarray(L0_attn_Wk, np.float32)),
        "wv": np.ascontiguousarray(np.asarray(L0_attn_Wv, np.float32)),
        "wo": np.ascontiguousarray(np.asarray(L0_attn_Wo, np.float32)),
        "wg": np.ascontiguousarray(np.asarray(L0_ffn_w_gate, np.float32)),
        "wu": np.ascontiguousarray(np.asarray(L0_ffn_w_up, np.float32)),
        "wd": np.ascontiguousarray(np.asarray(L0_ffn_w_down, np.float32)),
        "maskS": mS, "maskF": mF,
        "ident": np.eye(128, dtype=ml_dtypes.bfloat16),
    }
    in_maps = []
    for c in range(NCORES):
        m = dict(common)
        m["x"] = np.ascontiguousarray(x[c * SEQ_PER_CORE:(c + 1) * SEQ_PER_CORE])
        in_maps.append(m)
    res = run_bass_kernel_spmd(nc, in_maps, core_ids=list(range(NCORES)))
    out = np.concatenate([r["out"] for r in res.results], axis=0)
    return out.astype(np.float32)
